# version 20
# baseline (speedup 1.0000x reference)
from contextlib import ExitStack

import numpy as np
import concourse.bass as bass
import concourse.mybir as mybir
from concourse.bass_utils import run_bass_kernel_spmd

F32 = mybir.dt.float32
BF16 = mybir.dt.bfloat16
AF = mybir.ActivationFunctionType
ALU = mybir.AluOpType

P = 128
D = 1024
H = 16
DH = 64
T = 4096
B_CORE = 2
G = 1024
NG = (B_CORE * T) // G
NTILE = G // 512
NSUB = G // 128
NCH = G // 128
NE = 8
DFF = 4096
KB = 1024
DEBUG = False
STOP_AFTER = None


class Res:
    __slots__ = ("name", "writer", "readers")

    def __init__(self, name):
        self.name = name
        self.writer = None
        self.readers = []


class Trk:

    def __init__(self, regions=None):
        self.ops = []
        self.res = {}
        self.regions = regions or {}
        self.tenant = {}

    def R(self, name):
        r = self.res.get(name)
        if r is None:
            r = self.res[name] = Res(name)
        return r

    def emit(self, eng, fn, reads=(), writes=(), dma=False):
        reads = list(reads)
        writes = list(writes)
        for n in list(reads) + list(writes):
            tag = TENANT_TAG.get(n, n)
            for rg in self.regions.get(n, ()):
                if self.tenant.get(rg) != tag:
                    self.tenant[rg] = tag
                    writes.append(rg)
                else:
                    reads.append(rg)
        idx = len(self.ops)
        deps = set()
        rdeps = set()
        rres = [self.R(n) for n in reads]
        wres = [self.R(n) for n in writes]
        for r in rres:
            if r.writer is not None:
                deps.add(r.writer)
                rdeps.add(r.writer)
        for w in wres:
            if w.writer is not None:
                deps.add(w.writer)
            deps.update(w.readers)
        deps.discard(idx)
        rdeps.discard(idx)
        self.ops.append(dict(eng=eng, fn=fn, deps=sorted(deps), rdeps=rdeps,
                             dma=dma, r=reads, w=writes))
        for r in rres:
            r.readers.append(idx)
        for w in wres:
            w.writer = idx
            w.readers = []
        return idx

    def t(self, fn, **kw):
        return self.emit("t", fn, **kw)

    def v(self, fn, **kw):
        return self.emit("v", fn, **kw)

    def s(self, fn, **kw):
        return self.emit("s", fn, **kw)

    def g(self, fn, **kw):
        return self.emit("g", fn, **kw)

    def y(self, fn, **kw):
        return self.emit("y", fn, **kw)

    def build(self, block, sems):
        n = len(self.ops)
        eng_of = [op["eng"] for op in self.ops]
        is_dma = [op["dma"] for op in self.ops]
        need = [False] * n
        for op in self.ops:
            for d in op["deps"]:
                if is_dma[d] or eng_of[d] != op["eng"]:
                    need[d] = True
                elif op["eng"] in ("v", "s", "g") and d in op["rdeps"]:
                    need[d] = True
        val = [0] * n
        counts = {}
        for i, op in enumerate(self.ops):
            if is_dma[i]:
                k = "D" + op["eng"]
                counts[k] = counts.get(k, 0) + 16
                val[i] = counts[k]
            elif need[i]:
                k = "E" + op["eng"]
                counts[k] = counts.get(k, 0) + 1
                val[i] = counts[k]
        self.final_counts = counts

        def mk(eng):
            def body(e):
                hi = {}
                for i, op in enumerate(self.ops):
                    if op["eng"] != eng:
                        continue
                    for d in op["deps"]:
                        if (not is_dma[d] and eng_of[d] == eng
                                and not (eng in ("v", "s", "g") and d in op["rdeps"])):
                            continue
                        k = ("D" + eng_of[d]) if is_dma[d] else ("E" + eng_of[d])
                        v = val[d]
                        if v == 0 or hi.get(k, 0) >= v:
                            continue
                        hi[k] = v
                        e.wait_ge(sems[k], v)
                    inst = op["fn"](e)
                    if is_dma[i]:
                        inst.then_inc(sems["D" + eng], 16)
                    elif need[i]:
                        inst.then_inc(sems["E" + eng], 1)
            return body

        block.tensor(mk("t"))
        block.vector(mk("v"))
        block.scalar(mk("s"))
        block.gpsimd(mk("g"))
        block.sync(mk("y"))


REGIONS = {
    "wqkvT": ["R0", "R1", "R2"],
    "o_fm": ["R0"],
    "w2h": ["R0", "R1"],
    "woutT": ["R1"],
    "w1h": ["R2", "R3"],
    **{f"w1h_k{k}": ["R2", "R3"] for k in range(8)},
    **{f"w2h_q{q}": ["R0", "R1"] for q in range(4)},
    "x_tok": ["R3"],
    "q_sb": ["R4"],
    "k_sb": ["R5"],
    "acc": ["R4", "R5"],
    "v_sb": ["R6"],
    "hbuf": ["R6"],
}


TENANT_TAG = {}
for _k in range(8):
    TENANT_TAG[f"w1h_k{_k}"] = "w1h"
for _q in range(4):
    TENANT_TAG[f"w2h_q{_q}"] = "w2h"


def build_nc():
    nc = bass.Bass("TRN2", target_bir_lowering=False)

    x_h = nc.dram_tensor("x", [B_CORE, T, D], F32, kind="ExternalInput").ap()
    wqkvT_h = nc.dram_tensor("wqkvT", [D, 3 * D], F32, kind="ExternalInput").ap()
    wgT_h = nc.dram_tensor("wgT", [D, H], F32, kind="ExternalInput").ap()
    wbT_h = nc.dram_tensor("wbT", [D, H], F32, kind="ExternalInput").ap()
    bg_h = nc.dram_tensor("bg", [H, 1], F32, kind="ExternalInput").ap()
    bb_h = nc.dram_tensor("bb", [H, 1], F32, kind="ExternalInput").ap()
    woutT_h = nc.dram_tensor("woutT", [D, D], F32, kind="ExternalInput").ap()
    wr_h = nc.dram_tensor("wr", [D, NE], F32, kind="ExternalInput").ap()
    br_h = nc.dram_tensor("br", [1, NE], F32, kind="ExternalInput").ap()
    w1_h = nc.dram_tensor("w1", [NE, D, DFF], F32, kind="ExternalInput").ap()
    b1_h = nc.dram_tensor("b1", [NE, DFF], F32, kind="ExternalInput").ap()
    w2_h = nc.dram_tensor("w2", [NE, DFF, D], F32, kind="ExternalInput").ap()
    b2_h = nc.dram_tensor("b2", [NE, D], F32, kind="ExternalInput").ap()
    out_h = nc.dram_tensor("out", [B_CORE, T, D], F32, kind="ExternalOutput").ap()
    dbg = {}
    if DEBUG:
        for nm, shp, dt in [("d_xnfm", [P, 8 * G], F32), ("d_q", [P, 8 * G], F32),
                            ("d_k", [P, 8 * G], F32), ("d_v", [P, 8 * G], F32),
                            ("d_alpha", [H, G], F32), ("d_beta", [H, G], F32),
                            ("d_cl", [H, G], F32), ("d_ofm", [P, 8 * G], F32),
                            ("d_accC", [P, 8 * KB], F32), ("d_gates", [P, NSUB * NE], F32),
                            ("d_accD", [P, 8 * KB], F32), ("d_z", [P, NE * DH], F32), ("d_xtok", [P, 4 * KB], F32), ("d_lns", [P, 8], F32), ("d_otm", [P, H * DH], F32), ("d_et", [P, P], F32), ("d_y", [P, DH], F32), ("d_ofm0", [P, 8 * G], F32)]:
            dbg[nm] = nc.dram_tensor(nm, shp, dt, kind="ExternalOutput").ap()

    es = ExitStack()
    sbuf_total0 = nc.sbuf_bytes_remaining

    def sb(name, shape, dtype):
        return es.enter_context(nc.sbuf_tensor(name, shape, dtype))

    def ps(name, shape, dtype):
        return es.enter_context(nc.psum_tensor(name, shape, dtype))

    xn_fm = sb("xn_fm", [P, 8 * G], BF16)
    xn_tok = sb("xn_tok", [P, D], BF16)
    sel = sb("sel", [H, H * P], F32)
    maskT = sb("maskT", [2 * H, 16 * P], BF16)
    hm = sb("hm", [P, 16 * 32], BF16)
    ident_bf = sb("ident_bf", [P, P], BF16)
    identf = sb("identf", [P, P], F32)
    mb_incl = sb("mb_incl", [P, P], BF16)
    mb_strict = sb("mb_strict", [P, P], BF16)
    diag0 = sb("diag0", [P, P], BF16)
    zeros16 = sb("zeros16", [H, G], F32)
    ones1 = sb("ones1", [1, P], BF16)
    wgwbT = sb("wgwbT", [P, 8 * 2 * H], BF16)
    bg_sb = sb("bg_sb", [H, 1], F32)
    bb_sb = sb("bb_sb", [H, 1], F32)
    wr_sb = sb("wr_sb", [P, 8 * NE], BF16)
    br_bf = sb("br_bf", [1, NE], BF16)
    b2_sb = sb("b2_sb", [NE, D], F32)
    b1T = sb("b1T", [P, 32 * NE], F32)
    alpha = sb("alpha", [H, G], F32)
    beta = sb("beta", [H, G], F32)
    cl = sb("cl", [H, G], F32)
    st_clg2 = [sb(f"st_clg{i}", [H, P], F32) for i in range(2)]
    st_lam2 = [sb(f"st_lam{i}", [H, P], F32) for i in range(2)]
    st_gcb2 = [sb(f"st_gcb{i}", [H, P], F32) for i in range(2)]
    colT = [sb(f"colT{i}", [P, 64], F32) for i in range(2)]
    dcol = [sb(f"dcol{i}", [P, 5 * H], F32) for i in range(2)]
    negb = [sb(f"negb{i}", [P, H], F32) for i in range(2)]
    z_sb = sb("z_sb", [P, NE * DH], F32)
    zbf = sb("zbf", [P, NE * DH], BF16)
    kvt_ch2 = [sb(f"kvt_ch{i}", [P, 2 * H * DH], BF16) for i in range(2)]
    o_tm2 = [sb(f"o_tm{i}", [P, H * DH], BF16) for i in range(2)]
    sqs = sb("sqs", [P, 512], BF16)
    ss_sb = sb("ss_sb", [2 * H, 512], F32)
    nf_sb = sb("nf_sb", [2 * H, 512], F32)
    nf_bf = sb("nf_bf", [2 * H, 512], BF16)
    junk = sb("junk", [P, D], F32)
    xn_tok2 = sb("xn_tok2", [P, D], BF16)
    ln_scr = sb("ln_scr", [P, 8], F32)
    cc = sb("cc", [P, 4], F32)
    gates = sb("gates", [P, NSUB * NE], F32)
    g_scr = sb("g_scr", [P, 6 * NE + P], F32)
    g_col = sb("g_col", [P, 8], F32)
    Pincl = [sb(f"Pincl{i}", [P, P], BF16) for i in range(4)]
    Pstrict = [sb(f"Pstrict{i}", [P, P], BF16) for i in range(4)]
    ETb = [sb(f"ETb{i}", [P, P], BF16) for i in range(4)]
    Xa = [sb(f"Xa{i}", [P, 2 * P], BF16) for i in range(4)]
    Xb = [sb(f"Xb{i}", [P, 2 * P], BF16) for i in range(4)]
    Ysb = [sb(f"Ysb{i}", [P, DH], BF16) for i in range(4)]
    o2s = [sb(f"o2s{i}", [P, DH], F32) for i in range(4)]
    mlb = [sb(f"mlb{i}", [P, DH], BF16) for i in range(4)]

    pool_pp = 112 * KB
    left_ptr = nc._sbuf_addr_for_side("left")
    right_ptr = nc._sbuf_addr_for_side("right")
    base = (right_ptr - pool_pp) & ~63
    assert base >= left_ptr, f"pool overlaps fixed region: left={left_ptr} base={base}"

    def pool(name, shape, dtype, off_kb):
        return nc.alloc_sbuf_tensor_at(name, shape, dtype, offset=base + off_kb * KB)

    wqkvT = pool("wqkvT", [P, 8 * 3 * KB], BF16, 0)
    o_fm = pool("o_fm", [P, 8 * KB], BF16, 0)
    w2h = pool("w2h", [P, 16 * KB], BF16, 0)
    woutT = pool("woutT", [P, 8 * KB], BF16, 16)
    w1h = pool("w1h", [P, 16 * KB], BF16, 32)
    x_tok = pool("x_tok", [P, 4 * KB], F32, 48)
    q_pool = pool("q_pool", [P, 8 * KB], BF16, 64)
    acc = pool("acc", [P, 8 * KB], F32, 64)
    k_pool = pool("k_pool", [P, 8 * KB], BF16, 80)
    v_pool = pool("v_pool", [P, 8 * KB], BF16, 96)
    hbuf = pool("hbuf", [P, 8 * KB], BF16, 96)

    P0 = ps("P0", [P, 512], F32)
    P1 = ps("P1", [P, 512], F32)
    P2 = ps("P2", [P, 512], F32)
    P3 = ps("P3", [P, 512], F32)
    P4 = ps("P4", [P, 512], F32)
    P5 = ps("P5", [P, 512], F32)
    P6 = ps("P6", [P, 512], F32)
    P7 = ps("P7", [P, 512], F32)
    PB6 = P6.bitcast(BF16)
    PB7 = P7.bitcast(BF16)
    TBV = [P3.bitcast(BF16), P4.bitcast(BF16), P5.bitcast(BF16),
           P7.bitcast(BF16)]

    sems = {k: es.enter_context(nc.semaphore(k))
            for k in ("Et", "Ev", "Es", "Eg", "Dy", "Dg")}

    K = Trk(regions=REGIONS)

    def init_consts():
        K.g(lambda e: e.memset(ident_bf[:], 0.0), writes=["ident_bf"])
        K.g(lambda e: e.affine_select(out=ident_bf[:], in_=ident_bf[:],
                                      compare_op=ALU.not_equal, fill=1.0, base=0,
                                      pattern=[[-1, P]], channel_multiplier=1),
            reads=["ident_bf"], writes=["ident_bf"])
        K.g(lambda e: e.memset(identf[:], 0.0), writes=["identf"])
        K.g(lambda e: e.affine_select(out=identf[:], in_=identf[:],
                                      compare_op=ALU.not_equal, fill=1.0, base=0,
                                      pattern=[[-1, P]], channel_multiplier=1),
            reads=["identf"], writes=["identf"])
        K.g(lambda e: e.memset(mb_incl[:], 0.0), writes=["mb_incl"])
        K.g(lambda e: e.affine_select(out=mb_incl[:], in_=mb_incl[:],
                                      compare_op=ALU.is_ge, fill=-3e9, base=0,
                                      pattern=[[1, P]], channel_multiplier=-1),
            reads=["mb_incl"], writes=["mb_incl"])
        K.g(lambda e: e.memset(mb_strict[:], 0.0), writes=["mb_strict"])
        K.g(lambda e: e.affine_select(out=mb_strict[:], in_=mb_strict[:],
                                      compare_op=ALU.is_gt, fill=-3e9, base=0,
                                      pattern=[[1, P]], channel_multiplier=-1),
            reads=["mb_strict"], writes=["mb_strict"])
        K.g(lambda e: e.memset(diag0[:], 1.0), writes=["diag0"])
        K.g(lambda e: e.affine_select(out=diag0[:], in_=diag0[:],
                                      compare_op=ALU.not_equal, fill=0.0, base=0,
                                      pattern=[[-1, P]], channel_multiplier=1),
            reads=["diag0"], writes=["diag0"])
        K.g(lambda e: e.memset(ones1[:], 1.0), writes=["ones1"])
        K.g(lambda e: e.memset(cc[:, 0:1], 1e-5), writes=["cc"])
        K.g(lambda e: e.memset(cc[:, 1:2], -0.5), writes=["cc"])
        K.g(lambda e: e.memset(cc[:, 2:3], -1.0), writes=["cc"])
        K.g(lambda e: e.memset(zeros16[:], 0.0), writes=["zeros16"])
        K.g(lambda e: e.memset(sel[:], 0.0), writes=["sel"])
        K.g(lambda e: e.affine_select(
            out=sel[:].rearrange("r (h t) -> r h t", t=P),
            in_=sel[:].rearrange("r (h t) -> r h t", t=P),
            compare_op=ALU.not_equal, fill=1.0, base=0,
            pattern=[[1, H], [0, P]], channel_multiplier=-1),
            reads=["sel"], writes=["sel"])
        K.g(lambda e: e.memset(hm[:], 0.0), writes=["hm"])
        for sect in range(2):
            for j in range(8):
                vb = (sect * 8 + j) * 32
                for half in range(2):
                    col = vb + sect * H + 2 * j + half
                    lo, hi = half * 64, half * 64 + 64
                    K.g(lambda e, lo=lo, hi=hi, col=col:
                        e.memset(hm[lo:hi, col:col + 1], 1.0), writes=["hm"])
        K.g(lambda e: e.memset(maskT[:], 0.0), writes=["maskT"])
        K.g(lambda e: e.affine_select(
            out=maskT[:].rearrange("r (s j h q) -> r s j h q", s=2, j=8, h=2),
            in_=maskT[:].rearrange("r (s j h q) -> r s j h q", s=2, j=8, h=2),
            compare_op=ALU.not_equal, fill=1.0, base=0,
            pattern=[[16, 2], [2, 8], [1, 2], [0, 64]], channel_multiplier=-1),
            reads=["maskT"], writes=["maskT"])

    def init_small_weights():
        K.g(lambda e: e.dma_start(
            out=wgwbT[:].rearrange("p (k c) -> p k c", c=2 * H)[:, :, 0:H],
            in_=wgT_h.rearrange("(k p) c -> p k c", p=P)), dma=True, writes=["wgwbT"])
        K.g(lambda e: e.dma_start(
            out=wgwbT[:].rearrange("p (k c) -> p k c", c=2 * H)[:, :, H:2 * H],
            in_=wbT_h.rearrange("(k p) c -> p k c", p=P)), dma=True, writes=["wgwbT"])
        K.y(lambda e: e.dma_start(out=bg_sb[:], in_=bg_h), dma=True, writes=["bg_sb"])
        K.y(lambda e: e.dma_start(out=bb_sb[:], in_=bb_h), dma=True, writes=["bb_sb"])
        K.g(lambda e: e.dma_start(
            out=wr_sb[:].rearrange("p (k c) -> p k c", c=NE),
            in_=wr_h.rearrange("(k p) c -> p k c", p=P)), dma=True, writes=["wr_sb"])
        K.g(lambda e: e.dma_start(out=br_bf[:], in_=br_h), dma=True, writes=["br_bf"])
        K.y(lambda e: e.dma_start(out=b2_sb[:], in_=b2_h), dma=True, writes=["b2_sb"])
        K.y(lambda e: e.dma_start(out=acc[0:NE, 0:DFF], in_=b1_h), dma=True, writes=["acc"])
        for c in range(32):
            cc = c % 4
            K.t(lambda e, c=c, cc=cc: (e.transpose(
                P2[0:P, cc * NE:(cc + 1) * NE], acc[0:NE, c * P:(c + 1) * P],
                identf[0:NE, 0:NE]), e.drain())[1],
                reads=["acc", "identf"], writes=["P2"])
            K.v(lambda e, c=c, cc=cc: e.tensor_copy(
                out=b1T[:, c * NE:(c + 1) * NE], in_=P2[0:P, cc * NE:(cc + 1) * NE]),
                reads=["P2"], writes=["b1T"])
        K.g(lambda e: e.memset(z_sb[:], 0.0), writes=["z_sb"])

    def phaseA(g):
        seq, t0 = divmod(g * G, T)
        for k in range(8):
            K.g(lambda e, k=k: e.dma_start(out=wqkvT[:, k * 3072:(k + 1) * 3072],
                                           in_=wqkvT_h[k * P:(k + 1) * P, :]),
                dma=True, writes=["wqkvT"])
        def emit_qkv_tile(ti):
            tc = ti * 512
            for m in range(24):
                pm, pr = (P0, "P0") if m % 2 == 0 else (P1, "P1")
                for k in range(8):
                    K.t(lambda e, m=m, k=k, pm=pm, tc=tc: e.matmul(
                        pm[:], wqkvT[:, k * 3072 + m * P:k * 3072 + (m + 1) * P],
                        xn_fm[:, k * G + tc:k * G + tc + 512],
                        start=(k == 0), stop=(k == 7)),
                        reads=["wqkvT", "xn_fm"], writes=[pr])
                sect, j = divmod(m, 8)
                if sect == 2:
                    K.v(lambda e, j=j, pm=pm, tc=tc: e.tensor_copy(
                        out=v_pool[:, j * G + tc:j * G + tc + 512], in_=pm[:]),
                        reads=[pr], writes=["v_sb"])
                else:
                    qk = q_pool if sect == 0 else k_pool
                    qkr = "q_sb" if sect == 0 else "k_sb"
                    K.v(lambda e, j=j, pm=pm, tc=tc, qk=qk: e.tensor_copy(
                        out=qk[:, j * G + tc:j * G + tc + 512], in_=pm[:]),
                        reads=[pr], writes=[qkr])
                    K.s(lambda e, j=j, tc=tc, qk=qk: e.activation(
                        out=sqs[:, 0:512], in_=qk[:, j * G + tc:j * G + tc + 512],
                        func=AF.Square),
                        reads=[qkr], writes=["sqs"])
                    vb = (sect * 8 + j) * 32
                    K.t(lambda e, vb=vb, m=m: e.matmul(
                        P2[0:2 * H, 0:512], hm[:, vb:vb + 32], sqs[:, 0:512],
                        start=(m == 0), stop=(m == 15)),
                        reads=["sqs", "hm"], writes=["P2"])
            K.v(lambda e: e.tensor_scalar_max(out=ss_sb[:], in0=P2[0:2 * H, 0:512],
                                              scalar1=1e-24),
                reads=["P2"], writes=["ss_sb"])
            K.s(lambda e: e.activation(out=nf_sb[:], in_=ss_sb[:], func=AF.Ln),
                reads=["ss_sb"], writes=["nf_ln"])
            K.s(lambda e: e.activation(out=nf_bf[:], in_=nf_sb[:], func=AF.Exp,
                                       scale=cc[0:2 * H, 1:2]),
                reads=["nf_ln", "cc"], writes=["nf_bf"])
            for idx in range(16):
                sect, j = divmod(idx, 8)
                qk = q_pool if sect == 0 else k_pool
                qkr = "q_sb" if sect == 0 else "k_sb"
                K.t(lambda e, idx=idx: e.matmul(
                    P3[:], maskT[:, idx * P:(idx + 1) * P], nf_bf[:],
                    start=True, stop=True),
                    reads=["nf_bf", "maskT"], writes=["P3"])
                K.v(lambda e, j=j, tc=tc, qk=qk: e.tensor_tensor(
                    out=qk[:, j * G + tc:j * G + tc + 512],
                    in0=qk[:, j * G + tc:j * G + tc + 512], in1=P3[:], op=ALU.mult),
                    reads=["P3", qkr], writes=[qkr])
            for k in range(8):
                K.t(lambda e, k=k, tc=tc: e.matmul(
                    P4[0:H, 0:512], wgwbT[:, k * 2 * H:k * 2 * H + H],
                    xn_fm[:, k * G + tc:k * G + tc + 512],
                    start=(k == 0), stop=(k == 7)),
                    reads=["wgwbT", "xn_fm"], writes=["P4"])
                K.t(lambda e, k=k, tc=tc: e.matmul(
                    P5[0:H, 0:512], wgwbT[:, k * 2 * H + H:(k + 1) * 2 * H],
                    xn_fm[:, k * G + tc:k * G + tc + 512],
                    start=(k == 0), stop=(k == 7)),
                    reads=["wgwbT", "xn_fm"], writes=["P5"])
            K.s(lambda e, tc=tc: e.activation(out=alpha[:, tc:tc + 512],
                                              in_=P4[0:H, 0:512],
                                              func=AF.Sigmoid, bias=bg_sb[:]),
                reads=["P4", "bg_sb"], writes=["alpha"])
            K.s(lambda e, tc=tc: e.activation(out=beta[:, tc:tc + 512],
                                              in_=P5[0:H, 0:512],
                                              func=AF.Sigmoid, bias=bb_sb[:]),
                reads=["P5", "bb_sb"], writes=["beta"])

        for ti_x in range(NTILE):
          K.y(lambda e, seq=seq, t0=t0, ti_x=ti_x: e.dma_start(
            out=x_tok[:].rearrange("p (s d) -> p s d", d=D),
            in_=x_h[seq, t0 + ti_x * 512:t0 + (ti_x + 1) * 512, :].rearrange("(s p) d -> p s d", p=P)),
            dma=True, writes=["x_tok"])
          if DEBUG and g == 0 and ti_x == 0:
            K.g(lambda e: e.dma_start(out=dbg["d_xtok"], in_=x_tok[:]), dma=True, reads=["x_tok"])
          for sl in range(4):
            s = ti_x * 4 + sl
            xs = x_tok[:, sl * D:(sl + 1) * D]
            K.v(lambda e, xs=xs: e.reduce_sum(out=ln_scr[:, 0:1], in_=xs,
                                              axis=mybir.AxisListType.X),
                reads=["x_tok"], writes=["ln0"])
            K.s(lambda e, xs=xs: e.activation(
                out=junk[:, 0:D], in_=xs, func=AF.Square, accum_out=ln_scr[:, 1:2]),
                reads=["x_tok"], writes=["junk", "ln1"])
            K.v(lambda e: e.tensor_scalar_mul(out=ln_scr[:, 2:3], in0=ln_scr[:, 0:1],
                                              scalar1=1.0 / D),
                reads=["ln0"], writes=["ln2"])
            K.v(lambda e: e.tensor_tensor(out=ln_scr[:, 3:4], in0=ln_scr[:, 2:3],
                                          in1=ln_scr[:, 2:3], op=ALU.mult),
                reads=["ln2"], writes=["ln3"])
            K.v(lambda e: e.scalar_tensor_tensor(out=ln_scr[:, 4:5], in0=ln_scr[:, 1:2],
                                                 scalar=1.0 / D, in1=ln_scr[:, 3:4],
                                                 op0=ALU.mult, op1=ALU.subtract),
                reads=["ln1", "ln3"], writes=["ln4"])
            K.s(lambda e: e.activation(out=ln_scr[:, 5:6], in_=ln_scr[:, 4:5],
                                       func=AF.Ln, bias=cc[:, 0:1]),
                reads=["ln4", "cc"], writes=["ln5"])
            K.s(lambda e: e.activation(out=ln_scr[:, 6:7], in_=ln_scr[:, 5:6],
                                       func=AF.Exp, scale=cc[:, 1:2]),
                reads=["ln5", "cc"], writes=["ln6"])
            if DEBUG and g == 0 and s == 0:
                K.g(lambda e: e.dma_start(out=dbg["d_lns"], in_=ln_scr[:]), dma=True,
                    reads=["ln0", "ln1", "ln2", "ln3", "ln4", "ln5", "ln6"])
            xnt = xn_tok if s % 2 == 0 else xn_tok2
            xntr = "xn_tok" if s % 2 == 0 else "xn_tok2"
            K.v(lambda e, xs=xs, xnt=xnt: e.tensor_scalar(out=xnt[:, 0:D], in0=xs,
                                                 scalar1=ln_scr[:, 2:3],
                                                 scalar2=ln_scr[:, 6:7],
                                                 op0=ALU.subtract, op1=ALU.mult),
                reads=["x_tok", "ln2", "ln6"], writes=[xntr])
            for k in range(8):
                pbuf, pr = (PB6, "P6") if k % 2 == 0 else (PB7, "P7")
                K.t(lambda e, k=k, pbuf=pbuf, xnt=xnt: (e.transpose(
                    pbuf[:, 0:P], xnt[:, k * P:(k + 1) * P], ident_bf[:]), e.drain())[1],
                    reads=[xntr, "ident_bf"], writes=[pr])
                K.v(lambda e, k=k, s=s, pbuf=pbuf: e.tensor_copy(
                    out=xn_fm[:, k * G + s * P:k * G + (s + 1) * P], in_=pbuf[:, 0:P]),
                    reads=[pr], writes=["xn_fm"])
          if sl == 3:
            emit_qkv_tile(ti_x)

        K.s(lambda e: e.activation(out=junk[0:H, 0:G], in_=alpha[:], func=AF.Ln),
            reads=["alpha"], writes=["junk"])
        K.v(lambda e: e.tensor_tensor_scan(out=cl[:], data0=junk[0:H, 0:G],
                                           data1=zeros16[:], initial=0.0,
                                           op0=ALU.add, op1=ALU.add),
            reads=["junk", "zeros16"], writes=["cl"])

    def chunk_pre_small(g, c):
        cb = c % 2
        st_clg, st_lam, st_gcb = st_clg2[cb], st_lam2[cb], st_gcb2[cb]
        kvt_ch = kvt_ch2[cb]
        c0 = c * P
        if c == 0:
            prevcol, prr = zeros16[:, 0:1], "zeros16"
        else:
            prevcol, prr = cl[:, c0 - 1:c0], "cl"
        K.v(lambda e, c0=c0, prevcol=prevcol: e.tensor_scalar(
            out=st_clg[:], in0=cl[:, c0:c0 + P], scalar1=prevcol, scalar2=None,
            op0=ALU.subtract),
            reads=["cl", prr], writes=[f"st_clg{cb}"])
        K.v(lambda e: e.tensor_scalar(out=st_lam[:], in0=st_clg[:],
                                      scalar1=st_clg[:, P - 1:P], scalar2=None,
                                      op0=ALU.subtract),
            reads=[f"st_clg{cb}"], writes=[f"st_lam{cb}"])
        K.v(lambda e: e.tensor_scalar(out=st_gcb[:], in0=zeros16[:, 0:P],
                                      scalar1=st_clg[:, P - 1:P], scalar2=None,
                                      op0=ALU.add),
            reads=[f"st_clg{cb}", "zeros16"], writes=[f"st_gcb{cb}"])
        srcs = [(st_clg[:], f"st_clg{cb}"), (st_lam[:], f"st_lam{cb}"),
                (st_gcb[:], f"st_gcb{cb}"), (beta[:, c0:c0 + P], "beta")]
        for i, (sl, rn) in enumerate(srcs):
            K.t(lambda e, i=i, sl=sl: (e.transpose(
                P2[0:P, i * H:(i + 1) * H], sl, identf[0:H, 0:H]), e.drain())[1],
                reads=[rn, "identf"], writes=["P2"])
        K.v(lambda e, cb=cb: e.tensor_copy(out=colT[cb][:], in_=P2[0:P, 0:64]),
            reads=["P2"], writes=[f"colT{cb}"])
        dc = dcol[cb]
        dcr = f"dcol{cb}"
        ctr = f"colT{cb}"
        ct = colT[cb]
        K.v(lambda e, dc=dc, ct=ct: e.tensor_scalar_mul(
            out=dc[:, 0:H], in0=ct[:, 0:H], scalar1=-1.0),
            reads=[ctr], writes=[dcr])
        K.s(lambda e, dc=dc, ct=ct: e.activation(out=dc[:, H:2 * H], in_=ct[:, 0:H],
                                                 func=AF.Exp),
            reads=[ctr], writes=[dcr])
        K.v(lambda e, dc=dc: e.tensor_scalar_mul(
            out=dc[:, 2 * H:3 * H], in0=dc[:, H:2 * H], scalar1=-1.0),
            reads=[dcr], writes=[dcr])
        K.s(lambda e, dc=dc, ct=ct: e.activation(out=dc[:, 3 * H:4 * H],
                                                 in_=ct[:, H:2 * H],
                                                 func=AF.Exp, scale=cc[:, 2:3]),
            reads=[ctr, "cc"], writes=[dcr])
        K.v(lambda e, dc=dc, ct=ct: e.tensor_tensor(
            out=dc[:, 3 * H:4 * H], in0=dc[:, 3 * H:4 * H],
            in1=ct[:, 3 * H:4 * H], op=ALU.mult),
            reads=[dcr, ctr], writes=[dcr])
        K.s(lambda e, dc=dc, ct=ct: e.activation(out=dc[:, 4 * H:5 * H],
                                                 in_=ct[:, 2 * H:3 * H], func=AF.Exp),
            reads=[ctr], writes=[dcr])
        K.v(lambda e, cb=cb, ct=ct: e.tensor_scalar_mul(
            out=negb[cb][:], in0=ct[:, 3 * H:4 * H], scalar1=-1.0),
            reads=[ctr], writes=[f"negb{cb}"])

    def chunk_pre_kv(g, c):
        cb = c % 2
        kvt_ch = kvt_ch2[cb]
        c0 = c * P
        for j in range(8):
            pbuf, pr = (PB6, "P6") if j % 2 == 0 else (PB7, "P7")
            K.t(lambda e, j=j, pbuf=pbuf, c0=c0: (e.transpose(
                pbuf[:, 0:P], k_pool[:, j * G + c0:j * G + c0 + P], ident_bf[:]), e.drain())[0],
                reads=["k_sb", "ident_bf"], writes=[pr])
            K.t(lambda e, j=j, pbuf=pbuf, c0=c0: (e.transpose(
                pbuf[:, P:2 * P], v_pool[:, j * G + c0:j * G + c0 + P], ident_bf[:]), e.drain())[1],
                reads=["v_sb", "ident_bf"], writes=[pr])
            K.v(lambda e, j=j, pbuf=pbuf: e.tensor_copy(
                out=kvt_ch[:, j * 2 * P:(j + 1) * 2 * P], in_=pbuf[:, 0:2 * P]),
                reads=[pr], writes=[f"kvt_ch{cb}"])

    def dn_instance_steps(g, c, h):
        S = []
        cb = c % 2
        par = h % 4
        c0 = c * P
        j = h // 2
        pb = (h % 2) * 64
        zcol = (h // 2) * 64
        TA, TAr = [(P0, "P0"), (P1, "P1"), (P2, "P2"), (P6, "P6")][par]
        TB, TBr = [(P3, "P3"), (P4, "P4"), (P5, "P5"), (P7, "P7")][par]
        PBt, PBr = TBV[par], TBr
        kT = k_pool[pb:pb + 64, j * G + c0:j * G + c0 + P]
        qT = q_pool[pb:pb + 64, j * G + c0:j * G + c0 + P]
        dc = dcol[cb]
        dcr = f"dcol{cb}"
        st_clg = st_clg2[cb]
        kvt_ch = kvt_ch2[cb]
        o_tm = o_tm2[cb]
        negcl = dc[:, h:h + 1]
        Gamc = dc[:, H + h:H + h + 1]
        negGam = dc[:, 2 * H + h:2 * H + h + 1]
        lamb = dc[:, 3 * H + h:3 * H + h + 1]
        gamC = dc[:, 4 * H + h:4 * H + h + 1]
        bcol = colT[cb][:, 3 * H + h:3 * H + h + 1]
        nbcol = negb[cb][:, h:h + 1]
        Y = Ysb[par]
        Yr = f"Ysb{par}"
        Pi, Pir = Pincl[par], f"Pincl{par}"
        Pst, Pstr = Pstrict[par], f"Pstrict{par}"
        Et, Etr = ETb[par], f"ETb{par}"
        KK, KQ = TA[:, 0:P], TA[:, P:2 * P]
        CBi, CBs = TA[:, 2 * P:3 * P], TA[:, 3 * P:4 * P]
        S1, S2 = TA[:, 0:P], TA[:, P:2 * P]
        O1, QZ = TA[:, 2 * P:2 * P + 64], TA[:, 2 * P + 64:3 * P]
        ZU = TA[:, 3 * P:3 * P + 64]
        KZ, Yp = TB[:, 0:64], TB[:, 64:P]

        S.append(lambda: K.t(lambda e: e.matmul(KK, kT, kT, start=True, stop=True),
            reads=["k_sb"], writes=[TAr]))
        S.append(lambda: K.t(lambda e: e.matmul(KQ, kT, qT, start=True, stop=True),
            reads=["k_sb", "q_sb"], writes=[TAr]))
        S.append(lambda: K.t(lambda e: e.matmul(CBi, ident_bf[:], mb_incl[:],
                               start=True, stop=False),
            reads=["ident_bf", "mb_incl"], writes=[TAr]))
        S.append(lambda: K.t(lambda e, h=h: e.matmul(CBi, sel[:, h * P:(h + 1) * P], st_clg[:],
                                    start=False, stop=True),
            reads=["sel", f"st_clg{cb}"], writes=[TAr]))
        S.append(lambda: K.s(lambda e: e.activation(out=Pi[:], in_=CBi, func=AF.Exp, bias=negcl),
            reads=[TAr, dcr], writes=[Pir]))
        S.append(lambda: K.v(lambda e: e.tensor_tensor(out=Pst[:], in0=Pi[:], in1=diag0[:],
                                                       op=ALU.mult),
            reads=[Pir, "diag0"], writes=[Pstr]))
        S.append(lambda: K.v(lambda e: e.scalar_tensor_tensor(out=Et[:], in0=Pi[:], scalar=bcol,
                                             in1=KQ, op0=ALU.mult, op1=ALU.mult),
            reads=[Pir, TAr, f"colT{cb}"], writes=[Etr]))
        S.append(lambda: K.v(lambda e: e.scalar_tensor_tensor(out=Xa[par][:, 0:P], in0=Pst[:],
                                             scalar=nbcol, in1=KK,
                                             op0=ALU.mult, op1=ALU.mult),
            reads=[Pstr, TAr, f"negb{cb}"], writes=[f"Xa{par}"]))
        S.append(lambda: K.t(lambda e: e.matmul(KZ, kT, zbf[pb:pb + 64, zcol:zcol + 64],
                               start=True, stop=True),
            reads=["k_sb", "zbf"], writes=[TBr]))
        S.append(lambda: K.v(lambda e: e.scalar_tensor_tensor(out=Y[:], in0=KZ, scalar=negGam,
                                             in1=kvt_ch[:, (h // 2) * 2 * P + P + (h % 2) * 64:(h // 2) * 2 * P + P + (h % 2) * 64 + 64],
                                             op0=ALU.mult, op1=ALU.add),
            reads=[TBr, dcr, f"kvt_ch{cb}"], writes=[Yr]))
        S.append(lambda: K.t(lambda e: (e.transpose(PBt[:, 512:512 + P], Xa[par][:, 0:P], ident_bf[:]), e.drain())[1],
            reads=[f"Xa{par}", "ident_bf"], writes=[PBr]))
        S.append(lambda: K.v(lambda e: e.tensor_copy(out=Xa[par][:, P:2 * P], in_=PBt[:, 512:512 + P]),
            reads=[PBr], writes=[f"Xa{par}"]))
        S.append(lambda: K.t(lambda e: e.matmul(Yp, ident_bf[:], Y[:], start=True, stop=False),
            reads=["ident_bf", Yr], writes=[TBr]))
        X_cur, Xcr = Xa[par], f"Xa{par}"
        X_nxt, Xnr = Xb[par], f"Xb{par}"
        nlev = 7
        for lv in range(nlev):
            S.append(lambda X_cur=X_cur, Xcr=Xcr, lv=lv: K.t(lambda e, X_cur=X_cur, lv=lv: e.matmul(
                Yp, X_cur[:, 0:P], Y[:], start=False, stop=(lv == nlev - 1)),
                reads=[Xcr, Yr], writes=[TBr]))
            if par % 2 == 0:
                S.append(lambda: K.s(lambda e: e.activation(out=Y[:], in_=Yp, func=AF.Copy),
                    reads=[TBr], writes=[Yr]))
            else:
                S.append(lambda: K.v(lambda e: e.tensor_copy(out=Y[:], in_=Yp),
                    reads=[TBr], writes=[Yr]))
            if lv < nlev - 1:
                S.append(lambda X_cur=X_cur, Xcr=Xcr: K.t(lambda e, X_cur=X_cur: e.matmul(
                    S1, X_cur[:, P:2 * P], X_cur[:, 0:P], start=True, stop=True),
                    reads=[Xcr], writes=[TAr]))
                if lv < nlev - 2:
                    S.append(lambda X_cur=X_cur, Xcr=Xcr: K.t(lambda e, X_cur=X_cur: e.matmul(
                        S2, X_cur[:, 0:P], X_cur[:, P:2 * P], start=True, stop=True),
                        reads=[Xcr], writes=[TAr]))
                if par % 2 == 1:
                    S.append(lambda X_nxt=X_nxt, Xnr=Xnr: K.s(lambda e, X_nxt=X_nxt: e.activation(
                        out=X_nxt[:], in_=TA[:, 0:2 * P], func=AF.Copy),
                        reads=[TAr], writes=[Xnr]))
                else:
                    S.append(lambda X_nxt=X_nxt, Xnr=Xnr: K.v(lambda e, X_nxt=X_nxt: e.tensor_copy(
                        out=X_nxt[:], in_=TA[:, 0:2 * P]),
                        reads=[TAr], writes=[Xnr]))
                X_cur, X_nxt = X_nxt, X_cur
                Xcr, Xnr = Xnr, Xcr
        S.append(lambda: K.v(lambda e: e.tensor_scalar(out=mlb[par][:], in0=Y[:], scalar1=lamb,
                                      scalar2=None, op0=ALU.mult),
            reads=[Yr, dcr], writes=[f"mlb{par}"]))
        S.append(lambda: K.t(lambda e: e.matmul(O1, Et[:], Y[:], start=True, stop=True),
            reads=[Etr, Yr], writes=[TAr]))
        S.append(lambda: K.t(lambda e: e.matmul(QZ, qT, zbf[pb:pb + 64, zcol:zcol + 64],
                               start=True, stop=True),
            reads=["q_sb", "zbf"], writes=[TAr]))
        S.append(lambda: K.v(lambda e: e.tensor_scalar(out=o2s[par][:], in0=QZ, scalar1=Gamc,
                                      scalar2=None, op0=ALU.mult),
            reads=[TAr, dcr], writes=[f"o2s{par}"]))
        S.append(lambda: K.v(lambda e, h=h: e.tensor_tensor(out=o_tm[:, h * 64:(h + 1) * 64],
                                           in0=O1, in1=o2s[par][:], op=ALU.add),
            reads=[TAr, f"o2s{par}"], writes=[f"o_tm{cb}"]))
        if pb == 0:
            S.append(lambda: K.t(lambda e, h=h: e.matmul(TA[0:64, 3 * P:3 * P + 64],
                                        kvt_ch[:, (h // 2) * 2 * P + (h % 2) * 64:(h // 2) * 2 * P + (h % 2) * 64 + 64],
                                        mlb[par][:], start=True, stop=True),
                reads=[f"kvt_ch{cb}", f"mlb{par}"], writes=[TAr]))
        else:
            S.append(lambda: K.t(lambda e, h=h: e.matmul(TA[64:P, 3 * P:3 * P + 64],
                                        kvt_ch[:, (h // 2) * 2 * P + (h % 2) * 64:(h // 2) * 2 * P + (h % 2) * 64 + 64],
                                        mlb[par][:], start=True, stop=True,
                                        tile_position=(0, 64)),
                reads=[f"kvt_ch{cb}", f"mlb{par}"], writes=[TAr]))
        S.append(lambda: K.v(lambda e, pb=pb, zcol=zcol: e.scalar_tensor_tensor(
            out=z_sb[pb:pb + 64, zcol:zcol + 64],
            in0=z_sb[pb:pb + 64, zcol:zcol + 64],
            scalar=gamC[pb:pb + 64, :], in1=TA[pb:pb + 64, 3 * P:3 * P + 64],
            op0=ALU.mult, op1=ALU.add),
            reads=["z_sb", dcr, TAr], writes=["z_sb"]))
        return S

    def chunk_post(g, c):
        cb = c % 2
        o_tm = o_tm2[cb]
        c0 = c * P
        for j in range(8):
            pbuf, pr = (PB6, "P6") if j % 2 == 0 else (PB7, "P7")
            K.t(lambda e, j=j, pbuf=pbuf: (e.transpose(
                pbuf[:, 0:P], o_tm[:, j * P:(j + 1) * P], ident_bf[:]), e.drain())[1],
                reads=[f"o_tm{cb}", "ident_bf"], writes=[pr])
            K.s(lambda e, j=j, pbuf=pbuf, c0=c0: e.activation(
                out=o_fm[:, j * G + c0:j * G + c0 + P], in_=pbuf[:, 0:P], func=AF.Copy),
                reads=[pr], writes=["o_fm"])

    def phaseB(g):
        if g % (T // G) == 0:
            K.g(lambda e: e.memset(z_sb[:], 0.0), reads=["z_sb"], writes=["z_sb"])
        for c in range(NCH):
            if c == 0:
                chunk_pre_small(g, 0)
                chunk_pre_kv(g, 0)
            K.s(lambda e: e.activation(out=zbf[:], in_=z_sb[:], func=AF.Copy),
                reads=["z_sb"], writes=["zbf"])
            for gi, h0 in enumerate(range(0, H, 4)):
                hs = [hh for hh in range(h0, min(h0 + 4, H))]
                SL = [dn_instance_steps(g, c, hh) for hh in hs]
                n = max(len(s_) for s_ in SL)
                for i2 in range(n):
                    for s_ in SL:
                        if i2 < len(s_):
                            s_[i2]()
                if not DEBUG:
                    if gi == 0 and c >= 1:
                        chunk_post(g, c - 1)
                    if gi == 1 and c + 1 < NCH:
                        chunk_pre_small(g, c + 1)
                    if gi == 2 and c + 1 < NCH:
                        chunk_pre_kv(g, c + 1)
            if DEBUG:
                if g == 0 and c == 0:
                    K.g(lambda e: e.dma_start(out=dbg["d_otm"], in_=o_tm2[0][:]), dma=True, reads=["o_tm0"])
                chunk_post(g, c)
                if g == 0 and c == 0:
                    K.g(lambda e: e.dma_start(out=dbg["d_ofm0"], in_=o_fm[:]), dma=True, reads=["o_fm"])
                if c + 1 < NCH:
                    chunk_pre_small(g, c + 1)
                    chunk_pre_kv(g, c + 1)
        if not DEBUG:
            chunk_post(g, NCH - 1)

    def phaseC(g):
        for half in range(2):
            K.g(lambda e, half=half: e.dma_start(
                out=woutT[:].rearrange("p (k c) -> p k c", c=KB)[:, half * 4:(half + 1) * 4, :],
                in_=woutT_h.rearrange("(k p) c -> p k c", p=P)[:, half * 4:(half + 1) * 4, :]),
                dma=True, writes=["woutT"])
        for tb in range(NSUB):
            for oh in range(2):
                pm, pr = (P0, "P0") if (tb * 2 + oh) % 2 == 0 else (P1, "P1")
                for k in range(8):
                    K.t(lambda e, tb=tb, oh=oh, k=k, pm=pm: e.matmul(
                        pm[:], o_fm[:, k * G + tb * P:k * G + (tb + 1) * P],
                        woutT[:, k * KB + oh * 512:k * KB + (oh + 1) * 512],
                        start=(k == 0), stop=(k == 7)),
                        reads=["o_fm", "woutT"], writes=[pr])
                K.v(lambda e, tb=tb, oh=oh, pm=pm: e.tensor_copy(
                    out=acc[:, tb * KB + oh * 512:tb * KB + (oh + 1) * 512], in_=pm[:]),
                    reads=[pr], writes=["acc"])

    def router(g):
        for tb in range(NSUB):
            K.t(lambda e: e.matmul(P2[:, 0:NE], ones1[:], br_bf[:], start=True, stop=False),
                reads=["ones1", "br_bf"], writes=["P2"])
            for k in range(8):
                K.t(lambda e, tb=tb, k=k: e.matmul(
                    P2[:, 0:NE], xn_fm[:, k * G + tb * P:k * G + (tb + 1) * P],
                    wr_sb[:, k * NE:(k + 1) * NE], start=False, stop=(k == 7)),
                    reads=["xn_fm", "wr_sb"], writes=["P2"])
            K.v(lambda e: e.reduce_max(out=g_col[:, 0:1], in_=P2[:, 0:NE],
                                       axis=mybir.AxisListType.X),
                reads=["P2"], writes=["gc0"])
            K.v(lambda e: e.tensor_scalar_mul(out=g_col[:, 1:2], in0=g_col[:, 0:1],
                                              scalar1=-1.0),
                reads=["gc0"], writes=["gc1"])
            K.s(lambda e: e.activation(out=g_scr[:, 0:NE], in_=P2[:, 0:NE], func=AF.Exp,
                                       bias=g_col[:, 1:2]),
                reads=["P2", "gc1"], writes=["gE"])
            K.v(lambda e: e.reduce_sum(out=g_col[:, 2:3], in_=g_scr[:, 0:NE],
                                       axis=mybir.AxisListType.X),
                reads=["gE"], writes=["gc2"])
            K.v(lambda e: e.reduce_max(out=g_col[:, 3:4], in_=g_scr[:, 0:NE],
                                       axis=mybir.AxisListType.X),
                reads=["gE"], writes=["gc3"])
            K.v(lambda e: e.tensor_scalar(out=g_scr[:, NE:2 * NE], in0=g_scr[:, 0:NE],
                                          scalar1=g_col[:, 3:4], scalar2=None,
                                          op0=ALU.is_equal),
                reads=["gE", "gc3"], writes=["gm1"])
            K.v(lambda e: e.scalar_tensor_tensor(out=g_scr[:, 2 * NE:3 * NE],
                                                 in0=g_scr[:, NE:2 * NE], scalar=-1e30,
                                                 in1=g_scr[:, 0:NE],
                                                 op0=ALU.mult, op1=ALU.add),
                reads=["gm1", "gE"], writes=["gE2"])
            K.v(lambda e: e.reduce_max(out=g_col[:, 4:5], in_=g_scr[:, 2 * NE:3 * NE],
                                       axis=mybir.AxisListType.X),
                reads=["gE2"], writes=["gc4"])
            K.v(lambda e: e.tensor_scalar(out=g_scr[:, 3 * NE:4 * NE], in0=g_scr[:, 0:NE],
                                          scalar1=g_col[:, 4:5], scalar2=None,
                                          op0=ALU.is_ge),
                reads=["gE", "gc4"], writes=["gm2"])
            K.v(lambda e: e.tensor_tensor(out=g_scr[:, 4 * NE:5 * NE], in0=g_scr[:, 0:NE],
                                          in1=g_scr[:, 3 * NE:4 * NE], op=ALU.mult),
                reads=["gE", "gm2"], writes=["gkeep"])
            K.v(lambda e: e.reduce_sum(out=g_col[:, 5:6], in_=g_scr[:, 4 * NE:5 * NE],
                                       axis=mybir.AxisListType.X),
                reads=["gkeep"], writes=["gc5"])
            K.v(lambda e: e.scalar_tensor_tensor(out=g_col[:, 6:7], in0=g_col[:, 2:3],
                                                 scalar=1e-9, in1=g_col[:, 5:6],
                                                 op0=ALU.mult, op1=ALU.add),
                reads=["gc2", "gc5"], writes=["gc6"])
            K.v(lambda e: e.reciprocal(out=g_col[:, 7:8], in_=g_col[:, 6:7]),
                reads=["gc6"], writes=["gc7"])
            K.v(lambda e, tb=tb: e.tensor_scalar(out=gates[:, tb * NE:(tb + 1) * NE],
                                                 in0=g_scr[:, 4 * NE:5 * NE],
                                                 scalar1=g_col[:, 7:8], scalar2=None,
                                                 op0=ALU.mult),
                reads=["gkeep", "gc7"], writes=["gates"])
            K.t(lambda e, tb=tb: (e.transpose(P3[0:NE, 0:P],
                                             gates[:, tb * NE:(tb + 1) * NE], identf[:]), e.drain())[1],
                reads=["gates", "identf"], writes=["P3"])
            K.v(lambda e: e.tensor_copy(out=g_scr[0:NE, 6 * NE:6 * NE + P],
                                        in_=P3[0:NE, 0:P]),
                reads=["P3"], writes=["gTs"])
            for oh in range(2):
                K.t(lambda e, oh=oh: e.matmul(P4[:, 0:512],
                                              g_scr[0:NE, 6 * NE:6 * NE + P],
                                              b2_sb[:, oh * 512:(oh + 1) * 512],
                                              start=True, stop=True),
                    reads=["gTs", "b2_sb"], writes=["P4"])
                K.v(lambda e, tb=tb, oh=oh: e.tensor_tensor(
                    out=acc[:, tb * KB + oh * 512:tb * KB + (oh + 1) * 512],
                    in0=acc[:, tb * KB + oh * 512:tb * KB + (oh + 1) * 512],
                    in1=P4[:, 0:512], op=ALU.add),
                    reads=["P4", "acc"], writes=["acc"])

    def load_w1h(e_i, fh):
        for k in range(8):
            K.g(lambda e, e_i=e_i, fh=fh, k=k: e.dma_start(
                out=w1h[:, k * 2048:(k + 1) * 2048],
                in_=w1_h[e_i, k * P:(k + 1) * P, fh * 2048:(fh + 1) * 2048]),
                dma=True, writes=[f"w1h_k{k}"])

    def load_w2h(e_i, fh):
        for q in range(4):
            K.g(lambda e, e_i=e_i, fh=fh, q=q: e.dma_start(
                out=w2h[:].rearrange("p (c o) -> p c o", o=KB)[:, q * 4:(q + 1) * 4, :],
                in_=w2_h[e_i].rearrange("(c p) o -> p c o", p=P)[:, fh * 16 + q * 4:fh * 16 + (q + 1) * 4, :]),
                dma=True, writes=[f"w2h_q{q}"])

    def expertD(g, e_i):
        for fh in range(2):
            for ti in range(NTILE):
                tc = ti * 512
                for fc in range(16):
                    pm, pr = [(P0, "P0"), (P1, "P1"), (P5, "P5")][fc % 3]
                    for k in range(8):
                        K.t(lambda e, fc=fc, k=k, pm=pm, tc=tc: e.matmul(
                            pm[:], w1h[:, k * 2048 + fc * P:k * 2048 + (fc + 1) * P],
                            xn_fm[:, k * G + tc:k * G + tc + 512],
                            start=(k == 0), stop=(k == 7)),
                            reads=[f"w1h_k{k}", "xn_fm"], writes=[pr])
                    bcol = b1T[:, (fh * 16 + fc) * NE + e_i:(fh * 16 + fc) * NE + e_i + 1]
                    K.s(lambda e, fc=fc, pm=pm, bcol=bcol: e.activation(
                        out=hbuf[:, fc * 512:(fc + 1) * 512], in_=pm[:],
                        func=AF.Gelu, bias=bcol),
                        reads=[pr, "b1T"], writes=["hbuf"])
                for tbl in range(4):
                    tb = ti * 4 + tbl
                    for oh in range(2):
                        pm, pr = [(P2, "P2"), (P3, "P3"), (P4, "P4")][(tbl * 2 + oh) % 3]
                        for fc in range(16):
                            K.t(lambda e, fc=fc, tbl=tbl, oh=oh, pm=pm: e.matmul(
                                pm[:], hbuf[:, fc * 512 + tbl * P:fc * 512 + (tbl + 1) * P],
                                w2h[:, fc * KB + oh * 512:fc * KB + (oh + 1) * 512],
                                start=(fc == 0), stop=(fc == 15)),
                                reads=["hbuf", f"w2h_q{fc // 4}"], writes=[pr])
                        gc = gates[:, tb * NE + e_i:tb * NE + e_i + 1]
                        K.v(lambda e, tb=tb, oh=oh, pm=pm, gc=gc: e.scalar_tensor_tensor(
                            out=acc[:, tb * KB + oh * 512:tb * KB + (oh + 1) * 512],
                            in0=pm[:], scalar=gc,
                            in1=acc[:, tb * KB + oh * 512:tb * KB + (oh + 1) * 512],
                            op0=ALU.mult, op1=ALU.add),
                            reads=[pr, "gates", "acc"], writes=["acc"])
                if ti == NTILE - 1:
                    if fh == 0:
                        load_w1h(e_i, 1)
                    elif e_i < NE - 1:
                        load_w1h(e_i + 1, 0)
            if fh == 0:
                load_w2h(e_i, 1)
            elif e_i < NE - 1:
                load_w2h(e_i + 1, 0)

    def phaseD(g):
        router(g)
        for e_i in range(NE):
            expertD(g, e_i)

    def phaseE(g):
        seq, t0 = divmod(g * G, T)
        for ti_x in range(NTILE):
            K.y(lambda e, seq=seq, t0=t0, ti_x=ti_x: e.dma_start(
                out=x_tok[:].rearrange("p (s d) -> p s d", d=D),
                in_=x_h[seq, t0 + ti_x * 512:t0 + (ti_x + 1) * 512, :].rearrange("(s p) d -> p s d", p=P)),
                dma=True, writes=["x_tok"])
            for sl in range(4):
                s = ti_x * 4 + sl
                K.v(lambda e, s=s, sl=sl: e.tensor_tensor(
                    out=x_tok[:, sl * D:(sl + 1) * D], in0=x_tok[:, sl * D:(sl + 1) * D],
                    in1=acc[:, s * KB:(s + 1) * KB], op=ALU.add),
                    reads=["x_tok", "acc"], writes=["x_tok"])
            K.y(lambda e, seq=seq, t0=t0, ti_x=ti_x: e.dma_start(
                out=out_h[seq, t0 + ti_x * 512:t0 + (ti_x + 1) * 512, :].rearrange("(s p) d -> p s d", p=P),
                in_=x_tok[:].rearrange("p (s d) -> p s d", d=D)),
                dma=True, reads=["x_tok"], writes=["outh"])

    init_consts()
    init_small_weights()
    def dump(nm, ap, res):
        if DEBUG:
            K.g(lambda e: e.dma_start(out=dbg[nm], in_=ap), dma=True, reads=[res])

    for g in range(NG if STOP_AFTER is None else 1):
        phaseA(g)
        load_w1h(0, 0)
        if STOP_AFTER == "A":
            dump("d_xnfm", xn_fm[:], "xn_fm")
            dump("d_q", q_pool[:], "q_sb")
            dump("d_k", k_pool[:], "k_sb")
            dump("d_v", v_pool[:], "v_sb")
            dump("d_alpha", alpha[:], "alpha")
            dump("d_beta", beta[:], "beta")
            dump("d_cl", cl[:], "cl")
            break
        if g == 0:
            dump("d_xnfm", xn_fm[:], "xn_fm")
            dump("d_q", q_pool[:], "q_sb")
            dump("d_k", k_pool[:], "k_sb")
            dump("d_v", v_pool[:], "v_sb")
            dump("d_alpha", alpha[:], "alpha")
            dump("d_beta", beta[:], "beta")
            dump("d_cl", cl[:], "cl")
        phaseB(g)
        if g == 0:
            dump("d_ofm", o_fm[:], "o_fm")
            dump("d_z", z_sb[:], "z_sb")
        phaseC(g)
        load_w2h(0, 0)
        if g == 0:
            dump("d_accC", acc[:], "acc")
        phaseD(g)
        if g == 0:
            dump("d_gates", gates[:], "gates")
            dump("d_accD", acc[:], "acc")
        phaseE(g)
    K.v(lambda e: e.engine_nop(), reads=["outh"])

    with nc.Block() as block:
        K.build(block, sems)

    es.close()
    return nc


def kernel(**inputs):
    x = np.ascontiguousarray(inputs["L0_attn_x"], dtype=np.float32)
    wqkvT = np.ascontiguousarray(np.asarray(inputs["L0_attn_Wqkv"]).T)
    wgT = np.ascontiguousarray(np.asarray(inputs["L0_attn_Wg"]).T)
    wbT = np.ascontiguousarray(np.asarray(inputs["L0_attn_Wb"]).T)
    bg = np.ascontiguousarray(np.asarray(inputs["L0_attn_bg"]).reshape(H, 1))
    bb = np.ascontiguousarray(np.asarray(inputs["L0_attn_bb"]).reshape(H, 1))
    woutT = np.ascontiguousarray(np.asarray(inputs["L0_attn_Wout"]).T)
    wr = np.ascontiguousarray(inputs["L0_ffn_Wr"])
    br = np.ascontiguousarray(np.asarray(inputs["L0_ffn_br"]).reshape(1, NE))
    w1 = np.ascontiguousarray(inputs["L0_ffn_W1"])
    b1 = np.ascontiguousarray(inputs["L0_ffn_b1"])
    w2 = np.ascontiguousarray(inputs["L0_ffn_W2"])
    b2 = np.ascontiguousarray(inputs["L0_ffn_b2"])

    nc = build_nc()
    in_maps = []
    for c in range(8):
        in_maps.append({
            "x": np.ascontiguousarray(x[2 * c:2 * c + 2]),
            "wqkvT": wqkvT, "wgT": wgT, "wbT": wbT, "bg": bg, "bb": bb,
            "woutT": woutT, "wr": wr, "br": br,
            "w1": w1, "b1": b1, "w2": w2, "b2": b2,
        })
    res = run_bass_kernel_spmd(nc, in_maps, core_ids=list(range(8)))
    out = np.concatenate([np.asarray(r["out"]) for r in res.results], axis=0)
    return out.reshape(16, T, D).astype(np.float32)


# version 21
# speedup vs baseline: 1.0535x; 1.0535x over previous
from contextlib import ExitStack

import numpy as np
import concourse.bass as bass
import concourse.mybir as mybir
from concourse.bass_utils import run_bass_kernel_spmd

F32 = mybir.dt.float32
BF16 = mybir.dt.bfloat16
AF = mybir.ActivationFunctionType
ALU = mybir.AluOpType

P = 128
D = 1024
H = 16
DH = 64
T = 4096
B_CORE = 2
G = 1024
NG = (B_CORE * T) // G
NTILE = G // 512
NSUB = G // 128
NCH = G // 128
NE = 8
DFF = 4096
KB = 1024
DEBUG = False
STOP_AFTER = None


class Res:
    __slots__ = ("name", "writer", "readers")

    def __init__(self, name):
        self.name = name
        self.writer = None
        self.readers = []


class Trk:

    def __init__(self, regions=None):
        self.ops = []
        self.res = {}
        self.regions = regions or {}
        self.tenant = {}

    def R(self, name):
        r = self.res.get(name)
        if r is None:
            r = self.res[name] = Res(name)
        return r

    def emit(self, eng, fn, reads=(), writes=(), dma=False):
        reads = list(reads)
        writes = list(writes)
        for n in list(reads) + list(writes):
            tag = TENANT_TAG.get(n, n)
            for rg in self.regions.get(n, ()):
                if self.tenant.get(rg) != tag:
                    self.tenant[rg] = tag
                    writes.append(rg)
                else:
                    reads.append(rg)
        idx = len(self.ops)
        deps = set()
        rdeps = set()
        rres = [self.R(n) for n in reads]
        wres = [self.R(n) for n in writes]
        for r in rres:
            if r.writer is not None:
                deps.add(r.writer)
                rdeps.add(r.writer)
        for w in wres:
            if w.writer is not None:
                deps.add(w.writer)
            deps.update(w.readers)
        deps.discard(idx)
        rdeps.discard(idx)
        self.ops.append(dict(eng=eng, fn=fn, deps=sorted(deps), rdeps=rdeps,
                             dma=dma, r=reads, w=writes))
        for r in rres:
            r.readers.append(idx)
        for w in wres:
            w.writer = idx
            w.readers = []
        return idx

    def t(self, fn, **kw):
        return self.emit("t", fn, **kw)

    def v(self, fn, **kw):
        return self.emit("v", fn, **kw)

    def s(self, fn, **kw):
        return self.emit("s", fn, **kw)

    def g(self, fn, **kw):
        return self.emit("g", fn, **kw)

    def y(self, fn, **kw):
        return self.emit("y", fn, **kw)

    def build(self, block, sems):
        n = len(self.ops)
        eng_of = [op["eng"] for op in self.ops]
        is_dma = [op["dma"] for op in self.ops]
        need = [False] * n
        for op in self.ops:
            for d in op["deps"]:
                if is_dma[d] or eng_of[d] != op["eng"]:
                    need[d] = True
                elif op["eng"] in ("v", "s", "g") and d in op["rdeps"]:
                    need[d] = True
        val = [0] * n
        counts = {}
        for i, op in enumerate(self.ops):
            if is_dma[i]:
                k = "D" + op["eng"]
                counts[k] = counts.get(k, 0) + 16
                val[i] = counts[k]
            elif need[i]:
                k = "E" + op["eng"]
                counts[k] = counts.get(k, 0) + 1
                val[i] = counts[k]
        self.final_counts = counts

        def mk(eng):
            def body(e):
                hi = {}
                for i, op in enumerate(self.ops):
                    if op["eng"] != eng:
                        continue
                    for d in op["deps"]:
                        if (not is_dma[d] and eng_of[d] == eng
                                and not (eng in ("v", "s", "g") and d in op["rdeps"])):
                            continue
                        k = ("D" + eng_of[d]) if is_dma[d] else ("E" + eng_of[d])
                        v = val[d]
                        if v == 0 or hi.get(k, 0) >= v:
                            continue
                        hi[k] = v
                        e.wait_ge(sems[k], v)
                    inst = op["fn"](e)
                    if is_dma[i]:
                        inst.then_inc(sems["D" + eng], 16)
                    elif need[i]:
                        inst.then_inc(sems["E" + eng], 1)
            return body

        block.tensor(mk("t"))
        block.vector(mk("v"))
        block.scalar(mk("s"))
        block.gpsimd(mk("g"))
        block.sync(mk("y"))


REGIONS = {
    "wqkvT": ["R0", "R1", "R2"],
    "o_fm": ["R0"],
    "w2h": ["R0", "R1"],
    "woutT": ["R1"],
    "w1h": ["R2", "R3"],
    **{f"w1h_k{k}": ["R2", "R3"] for k in range(8)},
    **{f"w2h_q{q}": ["R0", "R1"] for q in range(4)},
    "x_tok": ["R3"],
    "q_sb": ["R4"],
    "k_sb": ["R5"],
    "acc": ["R4", "R5"],
    "v_sb": ["R6"],
    "hbuf": ["R6"],
}


TENANT_TAG = {}
for _k in range(8):
    TENANT_TAG[f"w1h_k{_k}"] = "w1h"
for _q in range(4):
    TENANT_TAG[f"w2h_q{_q}"] = "w2h"


def build_nc():
    nc = bass.Bass("TRN2", target_bir_lowering=False)

    x_h = nc.dram_tensor("x", [B_CORE, T, D], F32, kind="ExternalInput").ap()
    wqkvT_h = nc.dram_tensor("wqkvT", [D, 3 * D], F32, kind="ExternalInput").ap()
    wgT_h = nc.dram_tensor("wgT", [D, H], F32, kind="ExternalInput").ap()
    wbT_h = nc.dram_tensor("wbT", [D, H], F32, kind="ExternalInput").ap()
    bg_h = nc.dram_tensor("bg", [H, 1], F32, kind="ExternalInput").ap()
    bb_h = nc.dram_tensor("bb", [H, 1], F32, kind="ExternalInput").ap()
    woutT_h = nc.dram_tensor("woutT", [D, D], F32, kind="ExternalInput").ap()
    wr_h = nc.dram_tensor("wr", [D, NE], F32, kind="ExternalInput").ap()
    br_h = nc.dram_tensor("br", [1, NE], F32, kind="ExternalInput").ap()
    w1_h = nc.dram_tensor("w1", [NE, D, DFF], F32, kind="ExternalInput").ap()
    b1_h = nc.dram_tensor("b1", [NE, DFF], F32, kind="ExternalInput").ap()
    w2_h = nc.dram_tensor("w2", [NE, DFF, D], F32, kind="ExternalInput").ap()
    b2_h = nc.dram_tensor("b2", [NE, D], F32, kind="ExternalInput").ap()
    out_h = nc.dram_tensor("out", [B_CORE, T, D], F32, kind="ExternalOutput").ap()
    dbg = {}
    if DEBUG:
        for nm, shp, dt in [("d_xnfm", [P, 8 * G], F32), ("d_q", [P, 8 * G], F32),
                            ("d_k", [P, 8 * G], F32), ("d_v", [P, 8 * G], F32),
                            ("d_alpha", [H, G], F32), ("d_beta", [H, G], F32),
                            ("d_cl", [H, G], F32), ("d_ofm", [P, 8 * G], F32),
                            ("d_accC", [P, 8 * KB], F32), ("d_gates", [P, NSUB * NE], F32),
                            ("d_accD", [P, 8 * KB], F32), ("d_z", [P, NE * DH], F32), ("d_xtok", [P, 4 * KB], F32), ("d_lns", [P, 8], F32), ("d_otm", [P, H * DH], F32), ("d_et", [P, P], F32), ("d_y", [P, DH], F32), ("d_ofm0", [P, 8 * G], F32)]:
            dbg[nm] = nc.dram_tensor(nm, shp, dt, kind="ExternalOutput").ap()

    es = ExitStack()
    sbuf_total0 = nc.sbuf_bytes_remaining

    def sb(name, shape, dtype):
        return es.enter_context(nc.sbuf_tensor(name, shape, dtype))

    def ps(name, shape, dtype):
        return es.enter_context(nc.psum_tensor(name, shape, dtype))

    xn_fm = sb("xn_fm", [P, 8 * G], BF16)
    xn_tok = sb("xn_tok", [P, D], BF16)
    sel = sb("sel", [H, H * P], F32)
    maskT = sb("maskT", [2 * H, 16 * P], BF16)
    hm = sb("hm", [P, 16 * 32], BF16)
    ident_bf = sb("ident_bf", [P, P], BF16)
    identf = sb("identf", [P, P], F32)
    mb_incl = sb("mb_incl", [P, P], BF16)
    mb_strict = sb("mb_strict", [P, P], BF16)
    diag0 = sb("diag0", [P, P], BF16)
    zeros16 = sb("zeros16", [H, G], F32)
    ones1 = sb("ones1", [1, P], BF16)
    wgwbT = sb("wgwbT", [P, 8 * 2 * H], BF16)
    bg_sb = sb("bg_sb", [H, 1], F32)
    bb_sb = sb("bb_sb", [H, 1], F32)
    wr_sb = sb("wr_sb", [P, 8 * NE], BF16)
    br_bf = sb("br_bf", [1, NE], BF16)
    b2_sb = sb("b2_sb", [NE, D], F32)
    b1T = sb("b1T", [P, 32 * NE], F32)
    alpha = sb("alpha", [H, G], F32)
    beta = sb("beta", [H, G], F32)
    cl = sb("cl", [H, G], F32)
    st_clg2 = [sb(f"st_clg{i}", [H, P], F32) for i in range(2)]
    st_lam2 = [sb(f"st_lam{i}", [H, P], F32) for i in range(2)]
    st_gcb2 = [sb(f"st_gcb{i}", [H, P], F32) for i in range(2)]
    colT = [sb(f"colT{i}", [P, 64], F32) for i in range(2)]
    dcol = [sb(f"dcol{i}", [P, 5 * H], F32) for i in range(2)]
    negb = [sb(f"negb{i}", [P, H], F32) for i in range(2)]
    z_sb = sb("z_sb", [P, NE * DH], F32)
    zbf = sb("zbf", [P, NE * DH], BF16)
    kvt_ch2 = [sb(f"kvt_ch{i}", [P, 2 * H * DH], BF16) for i in range(2)]
    o_tm2 = [sb(f"o_tm{i}", [P, H * DH], BF16) for i in range(2)]
    sqs = sb("sqs", [P, 512], BF16)
    sqs2 = sb("sqs2", [P, 512], BF16)
    ss_sb = sb("ss_sb", [2 * H, 512], F32)
    nf_bf = sb("nf_bf", [2 * H, 512], BF16)
    junk = sb("junk", [P, D], F32)
    xn_tok2 = sb("xn_tok2", [P, D], BF16)
    ln_scr = sb("ln_scr", [P, 8], F32)
    cc = sb("cc", [P, 4], F32)
    gates = sb("gates", [P, NSUB * NE], F32)
    g_scr = sb("g_scr", [P, 6 * NE + P], F32)
    g_col = sb("g_col", [P, 8], F32)
    Pincl = [sb(f"Pincl{i}", [P, P], BF16) for i in range(4)]
    Pstrict = [sb(f"Pstrict{i}", [P, P], BF16) for i in range(4)]
    ETb = [sb(f"ETb{i}", [P, P], BF16) for i in range(4)]
    Xa = [sb(f"Xa{i}", [P, 2 * P], BF16) for i in range(4)]
    Xb = [sb(f"Xb{i}", [P, 2 * P], BF16) for i in range(4)]
    Ysb = [sb(f"Ysb{i}", [P, DH], BF16) for i in range(4)]
    o2s = [sb(f"o2s{i}", [P, DH], F32) for i in range(4)]
    mlb = [sb(f"mlb{i}", [P, DH], BF16) for i in range(4)]

    pool_pp = 112 * KB
    left_ptr = nc._sbuf_addr_for_side("left")
    right_ptr = nc._sbuf_addr_for_side("right")
    base = (right_ptr - pool_pp) & ~63
    assert base >= left_ptr, f"pool overlaps fixed region: left={left_ptr} base={base}"

    def pool(name, shape, dtype, off_kb):
        return nc.alloc_sbuf_tensor_at(name, shape, dtype, offset=base + off_kb * KB)

    wqkvT = pool("wqkvT", [P, 8 * 3 * KB], BF16, 0)
    o_fm = pool("o_fm", [P, 8 * KB], BF16, 0)
    w2h = pool("w2h", [P, 16 * KB], BF16, 0)
    woutT = pool("woutT", [P, 8 * KB], BF16, 16)
    w1h = pool("w1h", [P, 16 * KB], BF16, 32)
    x_tok = pool("x_tok", [P, 4 * KB], F32, 48)
    q_pool = pool("q_pool", [P, 8 * KB], BF16, 64)
    acc = pool("acc", [P, 8 * KB], F32, 64)
    k_pool = pool("k_pool", [P, 8 * KB], BF16, 80)
    v_pool = pool("v_pool", [P, 8 * KB], BF16, 96)
    hbuf = pool("hbuf", [P, 8 * KB], BF16, 96)

    P0 = ps("P0", [P, 512], F32)
    P1 = ps("P1", [P, 512], F32)
    P2 = ps("P2", [P, 512], F32)
    P3 = ps("P3", [P, 512], F32)
    P4 = ps("P4", [P, 512], F32)
    P5 = ps("P5", [P, 512], F32)
    P6 = ps("P6", [P, 512], F32)
    P7 = ps("P7", [P, 512], F32)
    PB6 = P6.bitcast(BF16)
    PB7 = P7.bitcast(BF16)
    TBV = [P3.bitcast(BF16), P4.bitcast(BF16), P5.bitcast(BF16),
           P7.bitcast(BF16)]

    sems = {k: es.enter_context(nc.semaphore(k))
            for k in ("Et", "Ev", "Es", "Eg", "Dy", "Dg")}

    K = Trk(regions=REGIONS)

    def init_consts():
        K.g(lambda e: e.memset(ident_bf[:], 0.0), writes=["ident_bf"])
        K.g(lambda e: e.affine_select(out=ident_bf[:], in_=ident_bf[:],
                                      compare_op=ALU.not_equal, fill=1.0, base=0,
                                      pattern=[[-1, P]], channel_multiplier=1),
            reads=["ident_bf"], writes=["ident_bf"])
        K.g(lambda e: e.memset(identf[:], 0.0), writes=["identf"])
        K.g(lambda e: e.affine_select(out=identf[:], in_=identf[:],
                                      compare_op=ALU.not_equal, fill=1.0, base=0,
                                      pattern=[[-1, P]], channel_multiplier=1),
            reads=["identf"], writes=["identf"])
        K.g(lambda e: e.memset(mb_incl[:], 0.0), writes=["mb_incl"])
        K.g(lambda e: e.affine_select(out=mb_incl[:], in_=mb_incl[:],
                                      compare_op=ALU.is_ge, fill=-3e9, base=0,
                                      pattern=[[1, P]], channel_multiplier=-1),
            reads=["mb_incl"], writes=["mb_incl"])
        K.g(lambda e: e.memset(mb_strict[:], 0.0), writes=["mb_strict"])
        K.g(lambda e: e.affine_select(out=mb_strict[:], in_=mb_strict[:],
                                      compare_op=ALU.is_gt, fill=-3e9, base=0,
                                      pattern=[[1, P]], channel_multiplier=-1),
            reads=["mb_strict"], writes=["mb_strict"])
        K.g(lambda e: e.memset(diag0[:], 1.0), writes=["diag0"])
        K.g(lambda e: e.affine_select(out=diag0[:], in_=diag0[:],
                                      compare_op=ALU.not_equal, fill=0.0, base=0,
                                      pattern=[[-1, P]], channel_multiplier=1),
            reads=["diag0"], writes=["diag0"])
        K.g(lambda e: e.memset(ones1[:], 1.0), writes=["ones1"])
        K.g(lambda e: e.memset(cc[:, 0:1], 1e-5), writes=["cc"])
        K.g(lambda e: e.memset(cc[:, 1:2], -0.5), writes=["cc"])
        K.g(lambda e: e.memset(cc[:, 2:3], -1.0), writes=["cc"])
        K.g(lambda e: e.memset(zeros16[:], 0.0), writes=["zeros16"])
        K.g(lambda e: e.memset(sel[:], 0.0), writes=["sel"])
        K.g(lambda e: e.affine_select(
            out=sel[:].rearrange("r (h t) -> r h t", t=P),
            in_=sel[:].rearrange("r (h t) -> r h t", t=P),
            compare_op=ALU.not_equal, fill=1.0, base=0,
            pattern=[[1, H], [0, P]], channel_multiplier=-1),
            reads=["sel"], writes=["sel"])
        K.g(lambda e: e.memset(hm[:], 0.0), writes=["hm"])
        for sect in range(2):
            for j in range(8):
                vb = (sect * 8 + j) * 32
                for half in range(2):
                    col = vb + sect * H + 2 * j + half
                    lo, hi = half * 64, half * 64 + 64
                    K.g(lambda e, lo=lo, hi=hi, col=col:
                        e.memset(hm[lo:hi, col:col + 1], 1.0), writes=["hm"])
        K.g(lambda e: e.memset(maskT[:], 0.0), writes=["maskT"])
        K.g(lambda e: e.affine_select(
            out=maskT[:].rearrange("r (s j h q) -> r s j h q", s=2, j=8, h=2),
            in_=maskT[:].rearrange("r (s j h q) -> r s j h q", s=2, j=8, h=2),
            compare_op=ALU.not_equal, fill=1.0, base=0,
            pattern=[[16, 2], [2, 8], [1, 2], [0, 64]], channel_multiplier=-1),
            reads=["maskT"], writes=["maskT"])

    def init_small_weights():
        K.g(lambda e: e.dma_start(
            out=wgwbT[:].rearrange("p (k c) -> p k c", c=2 * H)[:, :, 0:H],
            in_=wgT_h.rearrange("(k p) c -> p k c", p=P)), dma=True, writes=["wgwbT"])
        K.g(lambda e: e.dma_start(
            out=wgwbT[:].rearrange("p (k c) -> p k c", c=2 * H)[:, :, H:2 * H],
            in_=wbT_h.rearrange("(k p) c -> p k c", p=P)), dma=True, writes=["wgwbT"])
        K.y(lambda e: e.dma_start(out=bg_sb[:], in_=bg_h), dma=True, writes=["bg_sb"])
        K.y(lambda e: e.dma_start(out=bb_sb[:], in_=bb_h), dma=True, writes=["bb_sb"])
        K.g(lambda e: e.dma_start(
            out=wr_sb[:].rearrange("p (k c) -> p k c", c=NE),
            in_=wr_h.rearrange("(k p) c -> p k c", p=P)), dma=True, writes=["wr_sb"])
        K.g(lambda e: e.dma_start(out=br_bf[:], in_=br_h), dma=True, writes=["br_bf"])
        K.y(lambda e: e.dma_start(out=b2_sb[:], in_=b2_h), dma=True, writes=["b2_sb"])
        K.y(lambda e: e.dma_start(out=acc[0:NE, 0:DFF], in_=b1_h), dma=True, writes=["acc"])
        for c in range(32):
            cc = c % 4
            K.t(lambda e, c=c, cc=cc: (e.transpose(
                P2[0:P, cc * NE:(cc + 1) * NE], acc[0:NE, c * P:(c + 1) * P],
                identf[0:NE, 0:NE]), e.drain())[1],
                reads=["acc", "identf"], writes=["P2"])
            K.v(lambda e, c=c, cc=cc: e.tensor_copy(
                out=b1T[:, c * NE:(c + 1) * NE], in_=P2[0:P, cc * NE:(cc + 1) * NE]),
                reads=["P2"], writes=["b1T"])
        K.g(lambda e: e.memset(z_sb[:], 0.0), writes=["z_sb"])

    def phaseA(g):
        seq, t0 = divmod(g * G, T)
        for k in range(8):
            K.g(lambda e, k=k: e.dma_start(out=wqkvT[:, k * 3072:(k + 1) * 3072],
                                           in_=wqkvT_h[k * P:(k + 1) * P, :]),
                dma=True, writes=["wqkvT"])
        def emit_qkv_tile(ti):
            tc = ti * 512
            pend = None
            for m in range(24):
                pm, pr = (P0, "P0") if m % 2 == 0 else (P1, "P1")
                for k in range(8):
                    K.t(lambda e, m=m, k=k, pm=pm, tc=tc: e.matmul(
                        pm[:], wqkvT[:, k * 3072 + m * P:k * 3072 + (m + 1) * P],
                        xn_fm[:, k * G + tc:k * G + tc + 512],
                        start=(k == 0), stop=(k == 7)),
                        reads=["wqkvT", "xn_fm"], writes=[pr])
                if pend is not None:
                    vb0, sq0, sq0r, m0 = pend
                    pend = None
                    K.t(lambda e, vb0=vb0, sq0=sq0, m0=m0: e.matmul(
                        P2[0:2 * H, 0:512], hm[:, vb0:vb0 + 32], sq0[:, 0:512],
                        start=(m0 == 0), stop=(m0 == 15)),
                        reads=[sq0r, "hm"], writes=["P2"])
                sect, j = divmod(m, 8)
                if sect == 2:
                    K.v(lambda e, j=j, pm=pm, tc=tc: e.tensor_copy(
                        out=v_pool[:, j * G + tc:j * G + tc + 512], in_=pm[:]),
                        reads=[pr], writes=["v_sb"])
                else:
                    qk = q_pool if sect == 0 else k_pool
                    qkr = "q_sb" if sect == 0 else "k_sb"
                    K.v(lambda e, j=j, pm=pm, tc=tc, qk=qk: e.tensor_copy(
                        out=qk[:, j * G + tc:j * G + tc + 512], in_=pm[:]),
                        reads=[pr], writes=[qkr])
                    sq, sqr = (sqs, "sqs") if m % 2 == 0 else (sqs2, "sqs2")
                    K.s(lambda e, j=j, tc=tc, qk=qk, sq=sq: e.activation(
                        out=sq[:, 0:512], in_=qk[:, j * G + tc:j * G + tc + 512],
                        func=AF.Square),
                        reads=[qkr], writes=[sqr])
                    pend = ((sect * 8 + j) * 32, sq, sqr, m)
            K.v(lambda e: e.tensor_scalar_max(out=ss_sb[:], in0=P2[0:2 * H, 0:512],
                                              scalar1=1e-24),
                reads=["P2"], writes=["ss_sb"])
            K.s(lambda e: e.activation(out=junk[0:2 * H, 0:512], in_=ss_sb[:],
                                       func=AF.Ln),
                reads=["ss_sb"], writes=["junk"])
            K.s(lambda e: e.activation(out=nf_bf[:], in_=junk[0:2 * H, 0:512],
                                       func=AF.Exp, scale=cc[0:2 * H, 1:2]),
                reads=["junk", "cc"], writes=["nf_bf"])
            for idx in range(16):
                sect, j = divmod(idx, 8)
                qk = q_pool if sect == 0 else k_pool
                qkr = "q_sb" if sect == 0 else "k_sb"
                pmb, pmbr = (P3, "P3") if idx % 2 == 0 else (P4, "P4")
                K.t(lambda e, idx=idx, pmb=pmb: e.matmul(
                    pmb[:], maskT[:, idx * P:(idx + 1) * P], nf_bf[:],
                    start=True, stop=True),
                    reads=["nf_bf", "maskT"], writes=[pmbr])
                K.v(lambda e, j=j, tc=tc, qk=qk, pmb=pmb: e.tensor_tensor(
                    out=qk[:, j * G + tc:j * G + tc + 512],
                    in0=qk[:, j * G + tc:j * G + tc + 512], in1=pmb[:], op=ALU.mult),
                    reads=[pmbr, qkr], writes=[qkr])
            for k in range(8):
                K.t(lambda e, k=k, tc=tc: e.matmul(
                    P4[0:H, 0:512], wgwbT[:, k * 2 * H:k * 2 * H + H],
                    xn_fm[:, k * G + tc:k * G + tc + 512],
                    start=(k == 0), stop=(k == 7)),
                    reads=["wgwbT", "xn_fm"], writes=["P4"])
                K.t(lambda e, k=k, tc=tc: e.matmul(
                    P5[0:H, 0:512], wgwbT[:, k * 2 * H + H:(k + 1) * 2 * H],
                    xn_fm[:, k * G + tc:k * G + tc + 512],
                    start=(k == 0), stop=(k == 7)),
                    reads=["wgwbT", "xn_fm"], writes=["P5"])
            K.s(lambda e, tc=tc: e.activation(out=alpha[:, tc:tc + 512],
                                              in_=P4[0:H, 0:512],
                                              func=AF.Sigmoid, bias=bg_sb[:]),
                reads=["P4", "bg_sb"], writes=["alpha"])
            K.s(lambda e, tc=tc: e.activation(out=beta[:, tc:tc + 512],
                                              in_=P5[0:H, 0:512],
                                              func=AF.Sigmoid, bias=bb_sb[:]),
                reads=["P5", "bb_sb"], writes=["beta"])

        for ti_x in range(NTILE):
          K.y(lambda e, seq=seq, t0=t0, ti_x=ti_x: e.dma_start(
            out=x_tok[:].rearrange("p (s d) -> p s d", d=D),
            in_=x_h[seq, t0 + ti_x * 512:t0 + (ti_x + 1) * 512, :].rearrange("(s p) d -> p s d", p=P)),
            dma=True, writes=["x_tok"])
          if DEBUG and g == 0 and ti_x == 0:
            K.g(lambda e: e.dma_start(out=dbg["d_xtok"], in_=x_tok[:]), dma=True, reads=["x_tok"])
          for sl in range(4):
            s = ti_x * 4 + sl
            xs = x_tok[:, sl * D:(sl + 1) * D]
            K.v(lambda e, xs=xs: e.reduce_sum(out=ln_scr[:, 0:1], in_=xs,
                                              axis=mybir.AxisListType.X),
                reads=["x_tok"], writes=["ln0"])
            K.s(lambda e, xs=xs: e.activation(
                out=junk[:, 0:D], in_=xs, func=AF.Square, accum_out=ln_scr[:, 1:2]),
                reads=["x_tok"], writes=["junk", "ln1"])
            K.v(lambda e: e.tensor_scalar_mul(out=ln_scr[:, 2:3], in0=ln_scr[:, 0:1],
                                              scalar1=1.0 / D),
                reads=["ln0"], writes=["ln2"])
            K.v(lambda e: e.tensor_tensor(out=ln_scr[:, 3:4], in0=ln_scr[:, 2:3],
                                          in1=ln_scr[:, 2:3], op=ALU.mult),
                reads=["ln2"], writes=["ln3"])
            K.v(lambda e: e.scalar_tensor_tensor(out=ln_scr[:, 4:5], in0=ln_scr[:, 1:2],
                                                 scalar=1.0 / D, in1=ln_scr[:, 3:4],
                                                 op0=ALU.mult, op1=ALU.subtract),
                reads=["ln1", "ln3"], writes=["ln4"])
            K.s(lambda e: e.activation(out=ln_scr[:, 5:6], in_=ln_scr[:, 4:5],
                                       func=AF.Ln, bias=cc[:, 0:1]),
                reads=["ln4", "cc"], writes=["ln5"])
            K.s(lambda e: e.activation(out=ln_scr[:, 6:7], in_=ln_scr[:, 5:6],
                                       func=AF.Exp, scale=cc[:, 1:2]),
                reads=["ln5", "cc"], writes=["ln6"])
            if DEBUG and g == 0 and s == 0:
                K.g(lambda e: e.dma_start(out=dbg["d_lns"], in_=ln_scr[:]), dma=True,
                    reads=["ln0", "ln1", "ln2", "ln3", "ln4", "ln5", "ln6"])
            xnt = xn_tok if s % 2 == 0 else xn_tok2
            xntr = "xn_tok" if s % 2 == 0 else "xn_tok2"
            K.v(lambda e, xs=xs, xnt=xnt: e.tensor_scalar(out=xnt[:, 0:D], in0=xs,
                                                 scalar1=ln_scr[:, 2:3],
                                                 scalar2=ln_scr[:, 6:7],
                                                 op0=ALU.subtract, op1=ALU.mult),
                reads=["x_tok", "ln2", "ln6"], writes=[xntr])
            for k in range(8):
                pbuf, pr = (PB6, "P6") if k % 2 == 0 else (PB7, "P7")
                K.t(lambda e, k=k, pbuf=pbuf, xnt=xnt: (e.transpose(
                    pbuf[:, 0:P], xnt[:, k * P:(k + 1) * P], ident_bf[:]), e.drain())[1],
                    reads=[xntr, "ident_bf"], writes=[pr])
                K.v(lambda e, k=k, s=s, pbuf=pbuf: e.tensor_copy(
                    out=xn_fm[:, k * G + s * P:k * G + (s + 1) * P], in_=pbuf[:, 0:P]),
                    reads=[pr], writes=["xn_fm"])
          if sl == 3:
            emit_qkv_tile(ti_x)

        K.s(lambda e: e.activation(out=junk[0:H, 0:G], in_=alpha[:], func=AF.Ln),
            reads=["alpha"], writes=["junk"])
        K.v(lambda e: e.tensor_tensor_scan(out=cl[:], data0=junk[0:H, 0:G],
                                           data1=zeros16[:], initial=0.0,
                                           op0=ALU.add, op1=ALU.add),
            reads=["junk", "zeros16"], writes=["cl"])

    def chunk_pre(g, c):
        cb = c % 2
        st_clg, st_lam, st_gcb = st_clg2[cb], st_lam2[cb], st_gcb2[cb]
        kvt_ch = kvt_ch2[cb]
        c0 = c * P
        if c == 0:
            prevcol, prr = zeros16[:, 0:1], "zeros16"
        else:
            prevcol, prr = cl[:, c0 - 1:c0], "cl"
        K.v(lambda e, c0=c0, prevcol=prevcol: e.tensor_scalar(
            out=st_clg[:], in0=cl[:, c0:c0 + P], scalar1=prevcol, scalar2=None,
            op0=ALU.subtract),
            reads=["cl", prr], writes=[f"st_clg{cb}"])
        K.v(lambda e: e.tensor_scalar(out=st_lam[:], in0=st_clg[:],
                                      scalar1=st_clg[:, P - 1:P], scalar2=None,
                                      op0=ALU.subtract),
            reads=[f"st_clg{cb}"], writes=[f"st_lam{cb}"])
        K.v(lambda e: e.tensor_scalar(out=st_gcb[:], in0=zeros16[:, 0:P],
                                      scalar1=st_clg[:, P - 1:P], scalar2=None,
                                      op0=ALU.add),
            reads=[f"st_clg{cb}", "zeros16"], writes=[f"st_gcb{cb}"])
        srcs = [(st_clg[:], f"st_clg{cb}"), (st_lam[:], f"st_lam{cb}"),
                (st_gcb[:], f"st_gcb{cb}"), (beta[:, c0:c0 + P], "beta")]
        for i, (sl, rn) in enumerate(srcs):
            K.t(lambda e, i=i, sl=sl: (e.transpose(
                P2[0:P, i * H:(i + 1) * H], sl, identf[0:H, 0:H]), e.drain())[1],
                reads=[rn, "identf"], writes=["P2"])
        K.v(lambda e, cb=cb: e.tensor_copy(out=colT[cb][:], in_=P2[0:P, 0:64]),
            reads=["P2"], writes=[f"colT{cb}"])
        dc = dcol[cb]
        dcr = f"dcol{cb}"
        ctr = f"colT{cb}"
        ct = colT[cb]
        K.v(lambda e, dc=dc, ct=ct: e.tensor_scalar_mul(
            out=dc[:, 0:H], in0=ct[:, 0:H], scalar1=-1.0),
            reads=[ctr], writes=[dcr])
        K.s(lambda e, dc=dc, ct=ct: e.activation(out=dc[:, H:2 * H], in_=ct[:, 0:H],
                                                 func=AF.Exp),
            reads=[ctr], writes=[dcr])
        K.v(lambda e, dc=dc: e.tensor_scalar_mul(
            out=dc[:, 2 * H:3 * H], in0=dc[:, H:2 * H], scalar1=-1.0),
            reads=[dcr], writes=[dcr])
        K.s(lambda e, dc=dc, ct=ct: e.activation(out=dc[:, 3 * H:4 * H],
                                                 in_=ct[:, H:2 * H],
                                                 func=AF.Exp, scale=cc[:, 2:3]),
            reads=[ctr, "cc"], writes=[dcr])
        K.v(lambda e, dc=dc, ct=ct: e.tensor_tensor(
            out=dc[:, 3 * H:4 * H], in0=dc[:, 3 * H:4 * H],
            in1=ct[:, 3 * H:4 * H], op=ALU.mult),
            reads=[dcr, ctr], writes=[dcr])
        K.s(lambda e, dc=dc, ct=ct: e.activation(out=dc[:, 4 * H:5 * H],
                                                 in_=ct[:, 2 * H:3 * H], func=AF.Exp),
            reads=[ctr], writes=[dcr])
        K.v(lambda e, cb=cb, ct=ct: e.tensor_scalar_mul(
            out=negb[cb][:], in0=ct[:, 3 * H:4 * H], scalar1=-1.0),
            reads=[ctr], writes=[f"negb{cb}"])
        K.s(lambda e: e.activation(out=zbf[:], in_=z_sb[:], func=AF.Copy),
            reads=["z_sb"], writes=["zbf"])
        for j in range(8):
            pbuf, pr = (PB6, "P6") if j % 2 == 0 else (PB7, "P7")
            K.t(lambda e, j=j, pbuf=pbuf, c0=c0: (e.transpose(
                pbuf[:, 0:P], k_pool[:, j * G + c0:j * G + c0 + P], ident_bf[:]), e.drain())[0],
                reads=["k_sb", "ident_bf"], writes=[pr])
            K.t(lambda e, j=j, pbuf=pbuf, c0=c0: (e.transpose(
                pbuf[:, P:2 * P], v_pool[:, j * G + c0:j * G + c0 + P], ident_bf[:]), e.drain())[1],
                reads=["v_sb", "ident_bf"], writes=[pr])
            K.v(lambda e, j=j, pbuf=pbuf: e.tensor_copy(
                out=kvt_ch[:, j * 2 * P:(j + 1) * 2 * P], in_=pbuf[:, 0:2 * P]),
                reads=[pr], writes=[f"kvt_ch{cb}"])

    def dn_instance_steps(g, c, h):
        S = []
        cb = c % 2
        par = h % 4
        c0 = c * P
        j = h // 2
        pb = (h % 2) * 64
        zcol = (h // 2) * 64
        TA, TAr = [(P0, "P0"), (P1, "P1"), (P2, "P2"), (P6, "P6")][par]
        TB, TBr = [(P3, "P3"), (P4, "P4"), (P5, "P5"), (P7, "P7")][par]
        PBt, PBr = TBV[par], TBr
        kT = k_pool[pb:pb + 64, j * G + c0:j * G + c0 + P]
        qT = q_pool[pb:pb + 64, j * G + c0:j * G + c0 + P]
        dc = dcol[cb]
        dcr = f"dcol{cb}"
        st_clg = st_clg2[cb]
        kvt_ch = kvt_ch2[cb]
        o_tm = o_tm2[cb]
        negcl = dc[:, h:h + 1]
        Gamc = dc[:, H + h:H + h + 1]
        negGam = dc[:, 2 * H + h:2 * H + h + 1]
        lamb = dc[:, 3 * H + h:3 * H + h + 1]
        gamC = dc[:, 4 * H + h:4 * H + h + 1]
        bcol = colT[cb][:, 3 * H + h:3 * H + h + 1]
        nbcol = negb[cb][:, h:h + 1]
        Y = Ysb[par]
        Yr = f"Ysb{par}"
        Pi, Pir = Pincl[par], f"Pincl{par}"
        Pst, Pstr = Pstrict[par], f"Pstrict{par}"
        Et, Etr = ETb[par], f"ETb{par}"
        KK, KQ = TA[:, 0:P], TA[:, P:2 * P]
        CBi, CBs = TA[:, 2 * P:3 * P], TA[:, 3 * P:4 * P]
        S1, S2 = TA[:, 0:P], TA[:, P:2 * P]
        O1, QZ = TA[:, 2 * P:2 * P + 64], TA[:, 2 * P + 64:3 * P]
        ZU = TA[:, 3 * P:3 * P + 64]
        KZ, Yp = TB[:, 0:64], TB[:, 64:P]

        S.append(lambda: K.t(lambda e: e.matmul(KK, kT, kT, start=True, stop=True),
            reads=["k_sb"], writes=[TAr]))
        S.append(lambda: K.t(lambda e: e.matmul(KQ, kT, qT, start=True, stop=True),
            reads=["k_sb", "q_sb"], writes=[TAr]))
        S.append(lambda: K.t(lambda e: e.matmul(CBi, ident_bf[:], mb_incl[:],
                               start=True, stop=False),
            reads=["ident_bf", "mb_incl"], writes=[TAr]))
        S.append(lambda: K.t(lambda e, h=h: e.matmul(CBi, sel[:, h * P:(h + 1) * P], st_clg[:],
                                    start=False, stop=True),
            reads=["sel", f"st_clg{cb}"], writes=[TAr]))
        S.append(lambda: K.s(lambda e: e.activation(out=Pi[:], in_=CBi, func=AF.Exp, bias=negcl),
            reads=[TAr, dcr], writes=[Pir]))
        S.append(lambda: K.v(lambda e: e.tensor_tensor(out=Pst[:], in0=Pi[:], in1=diag0[:],
                                                       op=ALU.mult),
            reads=[Pir, "diag0"], writes=[Pstr]))
        S.append(lambda: K.v(lambda e: e.scalar_tensor_tensor(out=Et[:], in0=Pi[:], scalar=bcol,
                                             in1=KQ, op0=ALU.mult, op1=ALU.mult),
            reads=[Pir, TAr, f"colT{cb}"], writes=[Etr]))
        S.append(lambda: K.v(lambda e: e.scalar_tensor_tensor(out=Xa[par][:, 0:P], in0=Pst[:],
                                             scalar=nbcol, in1=KK,
                                             op0=ALU.mult, op1=ALU.mult),
            reads=[Pstr, TAr, f"negb{cb}"], writes=[f"Xa{par}"]))
        S.append(lambda: K.t(lambda e: e.matmul(KZ, kT, zbf[pb:pb + 64, zcol:zcol + 64],
                               start=True, stop=True),
            reads=["k_sb", "zbf"], writes=[TBr]))
        S.append(lambda: K.v(lambda e: e.scalar_tensor_tensor(out=Y[:], in0=KZ, scalar=negGam,
                                             in1=kvt_ch[:, (h // 2) * 2 * P + P + (h % 2) * 64:(h // 2) * 2 * P + P + (h % 2) * 64 + 64],
                                             op0=ALU.mult, op1=ALU.add),
            reads=[TBr, dcr, f"kvt_ch{cb}"], writes=[Yr]))
        S.append(lambda: K.t(lambda e: (e.transpose(PBt[:, 512:512 + P], Xa[par][:, 0:P], ident_bf[:]), e.drain())[1],
            reads=[f"Xa{par}", "ident_bf"], writes=[PBr]))
        S.append(lambda: K.v(lambda e: e.tensor_copy(out=Xa[par][:, P:2 * P], in_=PBt[:, 512:512 + P]),
            reads=[PBr], writes=[f"Xa{par}"]))
        S.append(lambda: K.t(lambda e: e.matmul(Yp, ident_bf[:], Y[:], start=True, stop=False),
            reads=["ident_bf", Yr], writes=[TBr]))
        X_cur, Xcr = Xa[par], f"Xa{par}"
        X_nxt, Xnr = Xb[par], f"Xb{par}"
        nlev = 7
        for lv in range(nlev):
            S.append(lambda X_cur=X_cur, Xcr=Xcr, lv=lv: K.t(lambda e, X_cur=X_cur, lv=lv: e.matmul(
                Yp, X_cur[:, 0:P], Y[:], start=False, stop=(lv == nlev - 1)),
                reads=[Xcr, Yr], writes=[TBr]))
            if par % 2 == 0:
                S.append(lambda: K.s(lambda e: e.activation(out=Y[:], in_=Yp, func=AF.Copy),
                    reads=[TBr], writes=[Yr]))
            else:
                S.append(lambda: K.v(lambda e: e.tensor_copy(out=Y[:], in_=Yp),
                    reads=[TBr], writes=[Yr]))
            if lv < nlev - 1:
                S.append(lambda X_cur=X_cur, Xcr=Xcr: K.t(lambda e, X_cur=X_cur: e.matmul(
                    S1, X_cur[:, P:2 * P], X_cur[:, 0:P], start=True, stop=True),
                    reads=[Xcr], writes=[TAr]))
                if lv < nlev - 2:
                    S.append(lambda X_cur=X_cur, Xcr=Xcr: K.t(lambda e, X_cur=X_cur: e.matmul(
                        S2, X_cur[:, 0:P], X_cur[:, P:2 * P], start=True, stop=True),
                        reads=[Xcr], writes=[TAr]))
                if par % 2 == 1:
                    S.append(lambda X_nxt=X_nxt, Xnr=Xnr: K.s(lambda e, X_nxt=X_nxt: e.activation(
                        out=X_nxt[:], in_=TA[:, 0:2 * P], func=AF.Copy),
                        reads=[TAr], writes=[Xnr]))
                else:
                    S.append(lambda X_nxt=X_nxt, Xnr=Xnr: K.v(lambda e, X_nxt=X_nxt: e.tensor_copy(
                        out=X_nxt[:], in_=TA[:, 0:2 * P]),
                        reads=[TAr], writes=[Xnr]))
                X_cur, X_nxt = X_nxt, X_cur
                Xcr, Xnr = Xnr, Xcr
        S.append(lambda: K.v(lambda e: e.tensor_scalar(out=mlb[par][:], in0=Y[:], scalar1=lamb,
                                      scalar2=None, op0=ALU.mult),
            reads=[Yr, dcr], writes=[f"mlb{par}"]))
        S.append(lambda: K.t(lambda e: e.matmul(O1, Et[:], Y[:], start=True, stop=True),
            reads=[Etr, Yr], writes=[TAr]))
        S.append(lambda: K.t(lambda e: e.matmul(QZ, qT, zbf[pb:pb + 64, zcol:zcol + 64],
                               start=True, stop=True),
            reads=["q_sb", "zbf"], writes=[TAr]))
        S.append(lambda: K.v(lambda e: e.tensor_scalar(out=o2s[par][:], in0=QZ, scalar1=Gamc,
                                      scalar2=None, op0=ALU.mult),
            reads=[TAr, dcr], writes=[f"o2s{par}"]))
        S.append(lambda: K.v(lambda e, h=h: e.tensor_tensor(out=o_tm[:, h * 64:(h + 1) * 64],
                                           in0=O1, in1=o2s[par][:], op=ALU.add),
            reads=[TAr, f"o2s{par}"], writes=[f"o_tm{cb}"]))
        if pb == 0:
            S.append(lambda: K.t(lambda e, h=h: e.matmul(TA[0:64, 3 * P:3 * P + 64],
                                        kvt_ch[:, (h // 2) * 2 * P + (h % 2) * 64:(h // 2) * 2 * P + (h % 2) * 64 + 64],
                                        mlb[par][:], start=True, stop=True),
                reads=[f"kvt_ch{cb}", f"mlb{par}"], writes=[TAr]))
        else:
            S.append(lambda: K.t(lambda e, h=h: e.matmul(TA[64:P, 3 * P:3 * P + 64],
                                        kvt_ch[:, (h // 2) * 2 * P + (h % 2) * 64:(h // 2) * 2 * P + (h % 2) * 64 + 64],
                                        mlb[par][:], start=True, stop=True,
                                        tile_position=(0, 64)),
                reads=[f"kvt_ch{cb}", f"mlb{par}"], writes=[TAr]))
        S.append(lambda: K.v(lambda e, pb=pb, zcol=zcol: e.scalar_tensor_tensor(
            out=z_sb[pb:pb + 64, zcol:zcol + 64],
            in0=z_sb[pb:pb + 64, zcol:zcol + 64],
            scalar=gamC[pb:pb + 64, :], in1=TA[pb:pb + 64, 3 * P:3 * P + 64],
            op0=ALU.mult, op1=ALU.add),
            reads=["z_sb", dcr, TAr], writes=["z_sb"]))
        return S

    def chunk_post(g, c):
        cb = c % 2
        o_tm = o_tm2[cb]
        c0 = c * P
        for j in range(8):
            pbuf, pr = (PB6, "P6") if j % 2 == 0 else (PB7, "P7")
            K.t(lambda e, j=j, pbuf=pbuf: (e.transpose(
                pbuf[:, 0:P], o_tm[:, j * P:(j + 1) * P], ident_bf[:]), e.drain())[1],
                reads=[f"o_tm{cb}", "ident_bf"], writes=[pr])
            K.s(lambda e, j=j, pbuf=pbuf, c0=c0: e.activation(
                out=o_fm[:, j * G + c0:j * G + c0 + P], in_=pbuf[:, 0:P], func=AF.Copy),
                reads=[pr], writes=["o_fm"])

    def phaseB(g):
        if g % (T // G) == 0:
            K.g(lambda e: e.memset(z_sb[:], 0.0), reads=["z_sb"], writes=["z_sb"])
        for c in range(NCH):
            chunk_pre(g, c)
            for h0 in range(0, H, 4):
                hs = [hh for hh in range(h0, min(h0 + 4, H))]
                SL = [dn_instance_steps(g, c, hh) for hh in hs]
                n = max(len(s_) for s_ in SL)
                for i2 in range(n):
                    for s_ in SL:
                        if i2 < len(s_):
                            s_[i2]()
            for h in []:
                if DEBUG and g == 0 and c == 0 and h == 4:
                    K.g(lambda e: e.dma_start(out=dbg["d_et"], in_=ETb[0][:]), dma=True, reads=["ETb0"])
                    K.g(lambda e: e.dma_start(out=dbg["d_y"], in_=Ysb[0][:]), dma=True, reads=["Ysb0"])
            if DEBUG and g == 0 and c == 0:
                K.g(lambda e: e.dma_start(out=dbg["d_otm"], in_=o_tm2[0][:]), dma=True, reads=["o_tm0"])
            chunk_post(g, c)
            if DEBUG and g == 0 and c == 0:
                K.g(lambda e: e.dma_start(out=dbg["d_ofm0"], in_=o_fm[:]), dma=True, reads=["o_fm"])

    def phaseC(g):
        for half in range(2):
            K.g(lambda e, half=half: e.dma_start(
                out=woutT[:].rearrange("p (k c) -> p k c", c=KB)[:, half * 4:(half + 1) * 4, :],
                in_=woutT_h.rearrange("(k p) c -> p k c", p=P)[:, half * 4:(half + 1) * 4, :]),
                dma=True, writes=["woutT"])
        for tb in range(NSUB):
            for oh in range(2):
                pm, pr = (P0, "P0") if (tb * 2 + oh) % 2 == 0 else (P1, "P1")
                for k in range(8):
                    K.t(lambda e, tb=tb, oh=oh, k=k, pm=pm: e.matmul(
                        pm[:], o_fm[:, k * G + tb * P:k * G + (tb + 1) * P],
                        woutT[:, k * KB + oh * 512:k * KB + (oh + 1) * 512],
                        start=(k == 0), stop=(k == 7)),
                        reads=["o_fm", "woutT"], writes=[pr])
                K.v(lambda e, tb=tb, oh=oh, pm=pm: e.tensor_copy(
                    out=acc[:, tb * KB + oh * 512:tb * KB + (oh + 1) * 512], in_=pm[:]),
                    reads=[pr], writes=["acc"])

    def router(g):
        def gates_tail(tb):
            K.t(lambda e, tb=tb: (e.transpose(P3[0:NE, 0:P],
                                             gates[:, tb * NE:(tb + 1) * NE], identf[:]), e.drain())[1],
                reads=["gates", "identf"], writes=["P3"])
            K.v(lambda e: e.tensor_copy(out=g_scr[0:NE, 6 * NE:6 * NE + P],
                                        in_=P3[0:NE, 0:P]),
                reads=["P3"], writes=["gTs"])
            for oh in range(2):
                K.t(lambda e, oh=oh: e.matmul(P4[:, 0:512],
                                              g_scr[0:NE, 6 * NE:6 * NE + P],
                                              b2_sb[:, oh * 512:(oh + 1) * 512],
                                              start=True, stop=True),
                    reads=["gTs", "b2_sb"], writes=["P4"])
                K.v(lambda e, tb=tb, oh=oh: e.tensor_tensor(
                    out=acc[:, tb * KB + oh * 512:tb * KB + (oh + 1) * 512],
                    in0=acc[:, tb * KB + oh * 512:tb * KB + (oh + 1) * 512],
                    in1=P4[:, 0:512], op=ALU.add),
                    reads=["P4", "acc"], writes=["acc"])

        for tb in range(NSUB):
            pl, plr = (P2, "P2") if tb % 2 == 0 else (P5, "P5")
            K.t(lambda e, pl=pl: e.matmul(pl[:, 0:NE], ones1[:], br_bf[:], start=True, stop=False),
                reads=["ones1", "br_bf"], writes=[plr])
            for k in range(8):
                K.t(lambda e, tb=tb, k=k, pl=pl: e.matmul(
                    pl[:, 0:NE], xn_fm[:, k * G + tb * P:k * G + (tb + 1) * P],
                    wr_sb[:, k * NE:(k + 1) * NE], start=False, stop=(k == 7)),
                    reads=["xn_fm", "wr_sb"], writes=[plr])
            if tb >= 1:
                gates_tail(tb - 1)
            K.v(lambda e, pl=pl: e.reduce_max(out=g_col[:, 0:1], in_=pl[:, 0:NE],
                                       axis=mybir.AxisListType.X),
                reads=[plr], writes=["gc0"])
            K.v(lambda e: e.tensor_scalar_mul(out=g_col[:, 1:2], in0=g_col[:, 0:1],
                                              scalar1=-1.0),
                reads=["gc0"], writes=["gc1"])
            K.s(lambda e, pl=pl: e.activation(out=g_scr[:, 0:NE], in_=pl[:, 0:NE], func=AF.Exp,
                                       bias=g_col[:, 1:2]),
                reads=[plr, "gc1"], writes=["gE"])
            K.v(lambda e: e.reduce_sum(out=g_col[:, 2:3], in_=g_scr[:, 0:NE],
                                       axis=mybir.AxisListType.X),
                reads=["gE"], writes=["gc2"])
            K.v(lambda e: e.reduce_max(out=g_col[:, 3:4], in_=g_scr[:, 0:NE],
                                       axis=mybir.AxisListType.X),
                reads=["gE"], writes=["gc3"])
            K.v(lambda e: e.tensor_scalar(out=g_scr[:, NE:2 * NE], in0=g_scr[:, 0:NE],
                                          scalar1=g_col[:, 3:4], scalar2=None,
                                          op0=ALU.is_equal),
                reads=["gE", "gc3"], writes=["gm1"])
            K.v(lambda e: e.scalar_tensor_tensor(out=g_scr[:, 2 * NE:3 * NE],
                                                 in0=g_scr[:, NE:2 * NE], scalar=-1e30,
                                                 in1=g_scr[:, 0:NE],
                                                 op0=ALU.mult, op1=ALU.add),
                reads=["gm1", "gE"], writes=["gE2"])
            K.v(lambda e: e.reduce_max(out=g_col[:, 4:5], in_=g_scr[:, 2 * NE:3 * NE],
                                       axis=mybir.AxisListType.X),
                reads=["gE2"], writes=["gc4"])
            K.v(lambda e: e.tensor_scalar(out=g_scr[:, 3 * NE:4 * NE], in0=g_scr[:, 0:NE],
                                          scalar1=g_col[:, 4:5], scalar2=None,
                                          op0=ALU.is_ge),
                reads=["gE", "gc4"], writes=["gm2"])
            K.v(lambda e: e.tensor_tensor(out=g_scr[:, 4 * NE:5 * NE], in0=g_scr[:, 0:NE],
                                          in1=g_scr[:, 3 * NE:4 * NE], op=ALU.mult),
                reads=["gE", "gm2"], writes=["gkeep"])
            K.v(lambda e: e.reduce_sum(out=g_col[:, 5:6], in_=g_scr[:, 4 * NE:5 * NE],
                                       axis=mybir.AxisListType.X),
                reads=["gkeep"], writes=["gc5"])
            K.v(lambda e: e.scalar_tensor_tensor(out=g_col[:, 6:7], in0=g_col[:, 2:3],
                                                 scalar=1e-9, in1=g_col[:, 5:6],
                                                 op0=ALU.mult, op1=ALU.add),
                reads=["gc2", "gc5"], writes=["gc6"])
            K.v(lambda e: e.reciprocal(out=g_col[:, 7:8], in_=g_col[:, 6:7]),
                reads=["gc6"], writes=["gc7"])
            K.v(lambda e, tb=tb: e.tensor_scalar(out=gates[:, tb * NE:(tb + 1) * NE],
                                                 in0=g_scr[:, 4 * NE:5 * NE],
                                                 scalar1=g_col[:, 7:8], scalar2=None,
                                                 op0=ALU.mult),
                reads=["gkeep", "gc7"], writes=["gates"])
        gates_tail(NSUB - 1)

    def load_w1h(e_i, fh):
        for k in range(8):
            K.g(lambda e, e_i=e_i, fh=fh, k=k: e.dma_start(
                out=w1h[:, k * 2048:(k + 1) * 2048],
                in_=w1_h[e_i, k * P:(k + 1) * P, fh * 2048:(fh + 1) * 2048]),
                dma=True, writes=[f"w1h_k{k}"])

    def load_w2h(e_i, fh):
        for q in range(4):
            K.g(lambda e, e_i=e_i, fh=fh, q=q: e.dma_start(
                out=w2h[:].rearrange("p (c o) -> p c o", o=KB)[:, q * 4:(q + 1) * 4, :],
                in_=w2_h[e_i].rearrange("(c p) o -> p c o", p=P)[:, fh * 16 + q * 4:fh * 16 + (q + 1) * 4, :]),
                dma=True, writes=[f"w2h_q{q}"])

    def expertD(g, e_i):
        for fh in range(2):
            for ti in range(NTILE):
                tc = ti * 512
                for fc in range(16):
                    pm, pr = [(P0, "P0"), (P1, "P1"), (P5, "P5")][fc % 3]
                    for k in range(8):
                        K.t(lambda e, fc=fc, k=k, pm=pm, tc=tc: e.matmul(
                            pm[:], w1h[:, k * 2048 + fc * P:k * 2048 + (fc + 1) * P],
                            xn_fm[:, k * G + tc:k * G + tc + 512],
                            start=(k == 0), stop=(k == 7)),
                            reads=[f"w1h_k{k}", "xn_fm"], writes=[pr])
                    bcol = b1T[:, (fh * 16 + fc) * NE + e_i:(fh * 16 + fc) * NE + e_i + 1]
                    K.s(lambda e, fc=fc, pm=pm, bcol=bcol: e.activation(
                        out=hbuf[:, fc * 512:(fc + 1) * 512], in_=pm[:],
                        func=AF.Gelu, bias=bcol),
                        reads=[pr, "b1T"], writes=["hbuf"])
                for tbl in range(4):
                    tb = ti * 4 + tbl
                    for oh in range(2):
                        pm, pr = [(P2, "P2"), (P3, "P3"), (P4, "P4")][(tbl * 2 + oh) % 3]
                        for fc in range(16):
                            K.t(lambda e, fc=fc, tbl=tbl, oh=oh, pm=pm: e.matmul(
                                pm[:], hbuf[:, fc * 512 + tbl * P:fc * 512 + (tbl + 1) * P],
                                w2h[:, fc * KB + oh * 512:fc * KB + (oh + 1) * 512],
                                start=(fc == 0), stop=(fc == 15)),
                                reads=["hbuf", f"w2h_q{fc // 4}"], writes=[pr])
                        gc = gates[:, tb * NE + e_i:tb * NE + e_i + 1]
                        K.v(lambda e, tb=tb, oh=oh, pm=pm, gc=gc: e.scalar_tensor_tensor(
                            out=acc[:, tb * KB + oh * 512:tb * KB + (oh + 1) * 512],
                            in0=pm[:], scalar=gc,
                            in1=acc[:, tb * KB + oh * 512:tb * KB + (oh + 1) * 512],
                            op0=ALU.mult, op1=ALU.add),
                            reads=[pr, "gates", "acc"], writes=["acc"])
                if ti == NTILE - 1:
                    if fh == 0:
                        load_w1h(e_i, 1)
                    elif e_i < NE - 1:
                        load_w1h(e_i + 1, 0)
            if fh == 0:
                load_w2h(e_i, 1)
            elif e_i < NE - 1:
                load_w2h(e_i + 1, 0)

    def phaseD(g):
        router(g)
        for e_i in range(NE):
            expertD(g, e_i)

    def phaseE(g):
        seq, t0 = divmod(g * G, T)
        for ti_x in range(NTILE):
            K.y(lambda e, seq=seq, t0=t0, ti_x=ti_x: e.dma_start(
                out=x_tok[:].rearrange("p (s d) -> p s d", d=D),
                in_=x_h[seq, t0 + ti_x * 512:t0 + (ti_x + 1) * 512, :].rearrange("(s p) d -> p s d", p=P)),
                dma=True, writes=["x_tok"])
            for sl in range(4):
                s = ti_x * 4 + sl
                K.v(lambda e, s=s, sl=sl: e.tensor_tensor(
                    out=x_tok[:, sl * D:(sl + 1) * D], in0=x_tok[:, sl * D:(sl + 1) * D],
                    in1=acc[:, s * KB:(s + 1) * KB], op=ALU.add),
                    reads=["x_tok", "acc"], writes=["x_tok"])
            K.y(lambda e, seq=seq, t0=t0, ti_x=ti_x: e.dma_start(
                out=out_h[seq, t0 + ti_x * 512:t0 + (ti_x + 1) * 512, :].rearrange("(s p) d -> p s d", p=P),
                in_=x_tok[:].rearrange("p (s d) -> p s d", d=D)),
                dma=True, reads=["x_tok"], writes=["outh"])

    init_consts()
    init_small_weights()
    def dump(nm, ap, res):
        if DEBUG:
            K.g(lambda e: e.dma_start(out=dbg[nm], in_=ap), dma=True, reads=[res])

    for g in range(NG if STOP_AFTER is None else 1):
        phaseA(g)
        load_w1h(0, 0)
        if STOP_AFTER == "A":
            dump("d_xnfm", xn_fm[:], "xn_fm")
            dump("d_q", q_pool[:], "q_sb")
            dump("d_k", k_pool[:], "k_sb")
            dump("d_v", v_pool[:], "v_sb")
            dump("d_alpha", alpha[:], "alpha")
            dump("d_beta", beta[:], "beta")
            dump("d_cl", cl[:], "cl")
            break
        if g == 0:
            dump("d_xnfm", xn_fm[:], "xn_fm")
            dump("d_q", q_pool[:], "q_sb")
            dump("d_k", k_pool[:], "k_sb")
            dump("d_v", v_pool[:], "v_sb")
            dump("d_alpha", alpha[:], "alpha")
            dump("d_beta", beta[:], "beta")
            dump("d_cl", cl[:], "cl")
        phaseB(g)
        if g == 0:
            dump("d_ofm", o_fm[:], "o_fm")
            dump("d_z", z_sb[:], "z_sb")
        phaseC(g)
        load_w2h(0, 0)
        if g == 0:
            dump("d_accC", acc[:], "acc")
        phaseD(g)
        if g == 0:
            dump("d_gates", gates[:], "gates")
            dump("d_accD", acc[:], "acc")
        phaseE(g)
    K.v(lambda e: e.engine_nop(), reads=["outh"])

    with nc.Block() as block:
        K.build(block, sems)

    es.close()
    return nc


def kernel(**inputs):
    x = np.ascontiguousarray(inputs["L0_attn_x"], dtype=np.float32)
    wqkvT = np.ascontiguousarray(np.asarray(inputs["L0_attn_Wqkv"]).T)
    wgT = np.ascontiguousarray(np.asarray(inputs["L0_attn_Wg"]).T)
    wbT = np.ascontiguousarray(np.asarray(inputs["L0_attn_Wb"]).T)
    bg = np.ascontiguousarray(np.asarray(inputs["L0_attn_bg"]).reshape(H, 1))
    bb = np.ascontiguousarray(np.asarray(inputs["L0_attn_bb"]).reshape(H, 1))
    woutT = np.ascontiguousarray(np.asarray(inputs["L0_attn_Wout"]).T)
    wr = np.ascontiguousarray(inputs["L0_ffn_Wr"])
    br = np.ascontiguousarray(np.asarray(inputs["L0_ffn_br"]).reshape(1, NE))
    w1 = np.ascontiguousarray(inputs["L0_ffn_W1"])
    b1 = np.ascontiguousarray(inputs["L0_ffn_b1"])
    w2 = np.ascontiguousarray(inputs["L0_ffn_W2"])
    b2 = np.ascontiguousarray(inputs["L0_ffn_b2"])

    nc = build_nc()
    in_maps = []
    for c in range(8):
        in_maps.append({
            "x": np.ascontiguousarray(x[2 * c:2 * c + 2]),
            "wqkvT": wqkvT, "wgT": wgT, "wbT": wbT, "bg": bg, "bb": bb,
            "woutT": woutT, "wr": wr, "br": br,
            "w1": w1, "b1": b1, "w2": w2, "b2": b2,
        })
    res = run_bass_kernel_spmd(nc, in_maps, core_ids=list(range(8)))
    out = np.concatenate([np.asarray(r["out"]) for r in res.results], axis=0)
    return out.reshape(16, T, D).astype(np.float32)


# version 22
# speedup vs baseline: 1.0757x; 1.0210x over previous
from contextlib import ExitStack

import numpy as np
import concourse.bass as bass
import concourse.mybir as mybir
from concourse.bass_utils import run_bass_kernel_spmd

F32 = mybir.dt.float32
BF16 = mybir.dt.bfloat16
AF = mybir.ActivationFunctionType
ALU = mybir.AluOpType

P = 128
D = 1024
H = 16
DH = 64
T = 4096
B_CORE = 2
G = 1024
NG = (B_CORE * T) // G
NTILE = G // 512
NSUB = G // 128
NCH = G // 128
NE = 8
DFF = 4096
KB = 1024
DEBUG = False
STOP_AFTER = None


class Res:
    __slots__ = ("name", "writer", "readers")

    def __init__(self, name):
        self.name = name
        self.writer = None
        self.readers = []


class Trk:

    def __init__(self, regions=None):
        self.ops = []
        self.res = {}
        self.regions = regions or {}
        self.tenant = {}

    def R(self, name):
        r = self.res.get(name)
        if r is None:
            r = self.res[name] = Res(name)
        return r

    def emit(self, eng, fn, reads=(), writes=(), dma=False):
        reads = list(reads)
        writes = list(writes)
        for n in list(reads) + list(writes):
            tag = TENANT_TAG.get(n, n)
            for rg in self.regions.get(n, ()):
                if self.tenant.get(rg) != tag:
                    self.tenant[rg] = tag
                    writes.append(rg)
                else:
                    reads.append(rg)
        idx = len(self.ops)
        deps = set()
        rdeps = set()
        rres = [self.R(n) for n in reads]
        wres = [self.R(n) for n in writes]
        for r in rres:
            if r.writer is not None:
                deps.add(r.writer)
                rdeps.add(r.writer)
        for w in wres:
            if w.writer is not None:
                deps.add(w.writer)
            deps.update(w.readers)
        deps.discard(idx)
        rdeps.discard(idx)
        self.ops.append(dict(eng=eng, fn=fn, deps=sorted(deps), rdeps=rdeps,
                             dma=dma, r=reads, w=writes))
        for r in rres:
            r.readers.append(idx)
        for w in wres:
            w.writer = idx
            w.readers = []
        return idx

    def t(self, fn, **kw):
        return self.emit("t", fn, **kw)

    def v(self, fn, **kw):
        return self.emit("v", fn, **kw)

    def s(self, fn, **kw):
        return self.emit("s", fn, **kw)

    def g(self, fn, **kw):
        return self.emit("g", fn, **kw)

    def y(self, fn, **kw):
        return self.emit("y", fn, **kw)

    def build(self, block, sems):
        n = len(self.ops)
        eng_of = [op["eng"] for op in self.ops]
        is_dma = [op["dma"] for op in self.ops]
        need = [False] * n
        for op in self.ops:
            for d in op["deps"]:
                if is_dma[d] or eng_of[d] != op["eng"]:
                    need[d] = True
                elif op["eng"] in ("v", "s", "g") and d in op["rdeps"]:
                    need[d] = True
        val = [0] * n
        counts = {}
        for i, op in enumerate(self.ops):
            if is_dma[i]:
                k = "D" + op["eng"]
                counts[k] = counts.get(k, 0) + 16
                val[i] = counts[k]
            elif need[i]:
                k = "E" + op["eng"]
                counts[k] = counts.get(k, 0) + 1
                val[i] = counts[k]
        self.final_counts = counts

        def mk(eng):
            def body(e):
                hi = {}
                for i, op in enumerate(self.ops):
                    if op["eng"] != eng:
                        continue
                    for d in op["deps"]:
                        if (not is_dma[d] and eng_of[d] == eng
                                and not (eng in ("v", "s", "g") and d in op["rdeps"])):
                            continue
                        k = ("D" + eng_of[d]) if is_dma[d] else ("E" + eng_of[d])
                        v = val[d]
                        if v == 0 or hi.get(k, 0) >= v:
                            continue
                        hi[k] = v
                        e.wait_ge(sems[k], v)
                    inst = op["fn"](e)
                    if is_dma[i]:
                        inst.then_inc(sems["D" + eng], 16)
                    elif need[i]:
                        inst.then_inc(sems["E" + eng], 1)
            return body

        block.tensor(mk("t"))
        block.vector(mk("v"))
        block.scalar(mk("s"))
        block.gpsimd(mk("g"))
        block.sync(mk("y"))


REGIONS = {
    "wqkvT": ["R0", "R1", "R2"],
    "o_fm": ["R0"],
    "woutT": ["R1"],
    **{f"w1qA_k{k}": ["R2"] for k in range(8)},
    **{f"w1qB_k{k}": ["R3"] for k in range(8)},
    **{f"w2qA_h{h}": ["R0"] for h in range(2)},
    **{f"w2qB_h{h}": ["R1"] for h in range(2)},
    "x_tok": ["R3"],
    "q_sb": ["R4"],
    "k_sb": ["R5"],
    "acc": ["R4", "R5"],
    "v_sb": ["R6"],
    "hbufA": ["R6"],
    "hbufB": ["R6"],
}


TENANT_TAG = {}
for _k in range(8):
    TENANT_TAG[f"w1qA_k{_k}"] = "w1qA"
    TENANT_TAG[f"w1qB_k{_k}"] = "w1qB"
for _h in range(2):
    TENANT_TAG[f"w2qA_h{_h}"] = "w2qA"
    TENANT_TAG[f"w2qB_h{_h}"] = "w2qB"
TENANT_TAG["hbufA"] = "hbuf"
TENANT_TAG["hbufB"] = "hbuf"


def build_nc():
    nc = bass.Bass("TRN2", target_bir_lowering=False)

    x_h = nc.dram_tensor("x", [B_CORE, T, D], F32, kind="ExternalInput").ap()
    wqkvT_h = nc.dram_tensor("wqkvT", [D, 3 * D], F32, kind="ExternalInput").ap()
    wgT_h = nc.dram_tensor("wgT", [D, H], F32, kind="ExternalInput").ap()
    wbT_h = nc.dram_tensor("wbT", [D, H], F32, kind="ExternalInput").ap()
    bg_h = nc.dram_tensor("bg", [H, 1], F32, kind="ExternalInput").ap()
    bb_h = nc.dram_tensor("bb", [H, 1], F32, kind="ExternalInput").ap()
    woutT_h = nc.dram_tensor("woutT", [D, D], F32, kind="ExternalInput").ap()
    wr_h = nc.dram_tensor("wr", [D, NE], F32, kind="ExternalInput").ap()
    br_h = nc.dram_tensor("br", [1, NE], F32, kind="ExternalInput").ap()
    w1_h = nc.dram_tensor("w1", [NE, D, DFF], F32, kind="ExternalInput").ap()
    b1_h = nc.dram_tensor("b1", [NE, DFF], F32, kind="ExternalInput").ap()
    w2_h = nc.dram_tensor("w2", [NE, DFF, D], F32, kind="ExternalInput").ap()
    b2_h = nc.dram_tensor("b2", [NE, D], F32, kind="ExternalInput").ap()
    out_h = nc.dram_tensor("out", [B_CORE, T, D], F32, kind="ExternalOutput").ap()
    dbg = {}
    if DEBUG:
        for nm, shp, dt in [("d_xnfm", [P, 8 * G], F32), ("d_q", [P, 8 * G], F32),
                            ("d_k", [P, 8 * G], F32), ("d_v", [P, 8 * G], F32),
                            ("d_alpha", [H, G], F32), ("d_beta", [H, G], F32),
                            ("d_cl", [H, G], F32), ("d_ofm", [P, 8 * G], F32),
                            ("d_accC", [P, 8 * KB], F32), ("d_gates", [P, NSUB * NE], F32),
                            ("d_accD", [P, 8 * KB], F32), ("d_z", [P, NE * DH], F32), ("d_xtok", [P, 4 * KB], F32), ("d_lns", [P, 8], F32), ("d_otm", [P, H * DH], F32), ("d_et", [P, P], F32), ("d_y", [P, DH], F32), ("d_ofm0", [P, 8 * G], F32)]:
            dbg[nm] = nc.dram_tensor(nm, shp, dt, kind="ExternalOutput").ap()

    es = ExitStack()
    sbuf_total0 = nc.sbuf_bytes_remaining

    def sb(name, shape, dtype):
        return es.enter_context(nc.sbuf_tensor(name, shape, dtype))

    def ps(name, shape, dtype):
        return es.enter_context(nc.psum_tensor(name, shape, dtype))

    xn_fm = sb("xn_fm", [P, 8 * G], BF16)
    xn_tok = sb("xn_tok", [P, D], BF16)
    sel = sb("sel", [H, H * P], F32)
    maskT = sb("maskT", [2 * H, 16 * P], BF16)
    hm = sb("hm", [P, 16 * 32], BF16)
    ident_bf = sb("ident_bf", [P, P], BF16)
    identf = sb("identf", [P, P], F32)
    mb_incl = sb("mb_incl", [P, P], BF16)
    mb_strict = sb("mb_strict", [P, P], BF16)
    diag0 = sb("diag0", [P, P], BF16)
    zeros16 = sb("zeros16", [H, G], F32)
    ones1 = sb("ones1", [1, P], BF16)
    wgwbT = sb("wgwbT", [P, 8 * 2 * H], BF16)
    bg_sb = sb("bg_sb", [H, 1], F32)
    bb_sb = sb("bb_sb", [H, 1], F32)
    wr_sb = sb("wr_sb", [P, 8 * NE], BF16)
    br_bf = sb("br_bf", [1, NE], BF16)
    b2_sb = sb("b2_sb", [NE, D], F32)
    b1T = sb("b1T", [P, 32 * NE], F32)
    alpha = sb("alpha", [H, G], F32)
    beta = sb("beta", [H, G], F32)
    cl = sb("cl", [H, G], F32)
    st_clg2 = [sb(f"st_clg{i}", [H, P], F32) for i in range(2)]
    st_lam2 = [sb(f"st_lam{i}", [H, P], F32) for i in range(2)]
    st_gcb2 = [sb(f"st_gcb{i}", [H, P], F32) for i in range(2)]
    colT = [sb(f"colT{i}", [P, 64], F32) for i in range(2)]
    dcol = [sb(f"dcol{i}", [P, 5 * H], F32) for i in range(2)]
    negb = [sb(f"negb{i}", [P, H], F32) for i in range(2)]
    z_sb = sb("z_sb", [P, NE * DH], F32)
    zbf = sb("zbf", [P, NE * DH], BF16)
    kvt_ch2 = [sb(f"kvt_ch{i}", [P, 2 * H * DH], BF16) for i in range(2)]
    o_tm2 = [sb(f"o_tm{i}", [P, H * DH], BF16) for i in range(2)]
    sqs = sb("sqs", [P, 512], BF16)
    sqs2 = sb("sqs2", [P, 512], BF16)
    ss_sb = sb("ss_sb", [2 * H, 512], F32)
    nf_bf = sb("nf_bf", [2 * H, 512], BF16)
    junk = sb("junk", [P, D], F32)
    xn_tok2 = sb("xn_tok2", [P, D], BF16)
    ln_scr = sb("ln_scr", [P, 8], F32)
    cc = sb("cc", [P, 4], F32)
    gates = sb("gates", [P, NSUB * NE], F32)
    g_scr = sb("g_scr", [P, 6 * NE + P], F32)
    g_col = sb("g_col", [P, 8], F32)
    Pincl = [sb(f"Pincl{i}", [P, P], BF16) for i in range(4)]
    Pstrict = [sb(f"Pstrict{i}", [P, P], BF16) for i in range(4)]
    ETb = [sb(f"ETb{i}", [P, P], BF16) for i in range(4)]
    Xa = [sb(f"Xa{i}", [P, 2 * P], BF16) for i in range(4)]
    Xb = [sb(f"Xb{i}", [P, 2 * P], BF16) for i in range(4)]
    Ysb = [sb(f"Ysb{i}", [P, DH], BF16) for i in range(4)]
    o2s = [sb(f"o2s{i}", [P, DH], F32) for i in range(4)]
    mlb = [sb(f"mlb{i}", [P, DH], BF16) for i in range(4)]

    pool_pp = 112 * KB
    left_ptr = nc._sbuf_addr_for_side("left")
    right_ptr = nc._sbuf_addr_for_side("right")
    base = (right_ptr - pool_pp) & ~63
    assert base >= left_ptr, f"pool overlaps fixed region: left={left_ptr} base={base}"

    def pool(name, shape, dtype, off_kb):
        return nc.alloc_sbuf_tensor_at(name, shape, dtype, offset=base + off_kb * KB)

    wqkvT = pool("wqkvT", [P, 8 * 3 * KB], BF16, 0)
    o_fm = pool("o_fm", [P, 8 * KB], BF16, 0)
    w2qA = pool("w2qA", [P, 8 * KB], BF16, 0)
    woutT = pool("woutT", [P, 8 * KB], BF16, 16)
    w2qB = pool("w2qB", [P, 8 * KB], BF16, 16)
    w1qA = pool("w1qA", [P, 8 * KB], BF16, 32)
    x_tok = pool("x_tok", [P, 4 * KB], F32, 48)
    w1qB = pool("w1qB", [P, 8 * KB], BF16, 48)
    q_pool = pool("q_pool", [P, 8 * KB], BF16, 64)
    acc = pool("acc", [P, 8 * KB], F32, 64)
    k_pool = pool("k_pool", [P, 8 * KB], BF16, 80)
    v_pool = pool("v_pool", [P, 8 * KB], BF16, 96)
    hbuf = pool("hbuf", [P, 8 * KB], BF16, 96)

    P0 = ps("P0", [P, 512], F32)
    P1 = ps("P1", [P, 512], F32)
    P2 = ps("P2", [P, 512], F32)
    P3 = ps("P3", [P, 512], F32)
    P4 = ps("P4", [P, 512], F32)
    P5 = ps("P5", [P, 512], F32)
    P6 = ps("P6", [P, 512], F32)
    P7 = ps("P7", [P, 512], F32)
    PB6 = P6.bitcast(BF16)
    PB7 = P7.bitcast(BF16)
    TBV = [P3.bitcast(BF16), P4.bitcast(BF16), P5.bitcast(BF16),
           P7.bitcast(BF16)]

    sems = {k: es.enter_context(nc.semaphore(k))
            for k in ("Et", "Ev", "Es", "Eg", "Dy", "Dg")}

    K = Trk(regions=REGIONS)

    def init_consts():
        K.g(lambda e: e.memset(ident_bf[:], 0.0), writes=["ident_bf"])
        K.g(lambda e: e.affine_select(out=ident_bf[:], in_=ident_bf[:],
                                      compare_op=ALU.not_equal, fill=1.0, base=0,
                                      pattern=[[-1, P]], channel_multiplier=1),
            reads=["ident_bf"], writes=["ident_bf"])
        K.g(lambda e: e.memset(identf[:], 0.0), writes=["identf"])
        K.g(lambda e: e.affine_select(out=identf[:], in_=identf[:],
                                      compare_op=ALU.not_equal, fill=1.0, base=0,
                                      pattern=[[-1, P]], channel_multiplier=1),
            reads=["identf"], writes=["identf"])
        K.g(lambda e: e.memset(mb_incl[:], 0.0), writes=["mb_incl"])
        K.g(lambda e: e.affine_select(out=mb_incl[:], in_=mb_incl[:],
                                      compare_op=ALU.is_ge, fill=-3e9, base=0,
                                      pattern=[[1, P]], channel_multiplier=-1),
            reads=["mb_incl"], writes=["mb_incl"])
        K.g(lambda e: e.memset(mb_strict[:], 0.0), writes=["mb_strict"])
        K.g(lambda e: e.affine_select(out=mb_strict[:], in_=mb_strict[:],
                                      compare_op=ALU.is_gt, fill=-3e9, base=0,
                                      pattern=[[1, P]], channel_multiplier=-1),
            reads=["mb_strict"], writes=["mb_strict"])
        K.g(lambda e: e.memset(diag0[:], 1.0), writes=["diag0"])
        K.g(lambda e: e.affine_select(out=diag0[:], in_=diag0[:],
                                      compare_op=ALU.not_equal, fill=0.0, base=0,
                                      pattern=[[-1, P]], channel_multiplier=1),
            reads=["diag0"], writes=["diag0"])
        K.g(lambda e: e.memset(ones1[:], 1.0), writes=["ones1"])
        K.g(lambda e: e.memset(cc[:, 0:1], 1e-5), writes=["cc"])
        K.g(lambda e: e.memset(cc[:, 1:2], -0.5), writes=["cc"])
        K.g(lambda e: e.memset(cc[:, 2:3], -1.0), writes=["cc"])
        K.g(lambda e: e.memset(zeros16[:], 0.0), writes=["zeros16"])
        K.g(lambda e: e.memset(sel[:], 0.0), writes=["sel"])
        K.g(lambda e: e.affine_select(
            out=sel[:].rearrange("r (h t) -> r h t", t=P),
            in_=sel[:].rearrange("r (h t) -> r h t", t=P),
            compare_op=ALU.not_equal, fill=1.0, base=0,
            pattern=[[1, H], [0, P]], channel_multiplier=-1),
            reads=["sel"], writes=["sel"])
        K.g(lambda e: e.memset(hm[:], 0.0), writes=["hm"])
        for sect in range(2):
            for j in range(8):
                vb = (sect * 8 + j) * 32
                for half in range(2):
                    col = vb + sect * H + 2 * j + half
                    lo, hi = half * 64, half * 64 + 64
                    K.g(lambda e, lo=lo, hi=hi, col=col:
                        e.memset(hm[lo:hi, col:col + 1], 1.0), writes=["hm"])
        K.g(lambda e: e.memset(maskT[:], 0.0), writes=["maskT"])
        K.g(lambda e: e.affine_select(
            out=maskT[:].rearrange("r (s j h q) -> r s j h q", s=2, j=8, h=2),
            in_=maskT[:].rearrange("r (s j h q) -> r s j h q", s=2, j=8, h=2),
            compare_op=ALU.not_equal, fill=1.0, base=0,
            pattern=[[16, 2], [2, 8], [1, 2], [0, 64]], channel_multiplier=-1),
            reads=["maskT"], writes=["maskT"])

    def init_small_weights():
        K.g(lambda e: e.dma_start(
            out=wgwbT[:].rearrange("p (k c) -> p k c", c=2 * H)[:, :, 0:H],
            in_=wgT_h.rearrange("(k p) c -> p k c", p=P)), dma=True, writes=["wgwbT"])
        K.g(lambda e: e.dma_start(
            out=wgwbT[:].rearrange("p (k c) -> p k c", c=2 * H)[:, :, H:2 * H],
            in_=wbT_h.rearrange("(k p) c -> p k c", p=P)), dma=True, writes=["wgwbT"])
        K.y(lambda e: e.dma_start(out=bg_sb[:], in_=bg_h), dma=True, writes=["bg_sb"])
        K.y(lambda e: e.dma_start(out=bb_sb[:], in_=bb_h), dma=True, writes=["bb_sb"])
        K.g(lambda e: e.dma_start(
            out=wr_sb[:].rearrange("p (k c) -> p k c", c=NE),
            in_=wr_h.rearrange("(k p) c -> p k c", p=P)), dma=True, writes=["wr_sb"])
        K.g(lambda e: e.dma_start(out=br_bf[:], in_=br_h), dma=True, writes=["br_bf"])
        K.y(lambda e: e.dma_start(out=b2_sb[:], in_=b2_h), dma=True, writes=["b2_sb"])
        K.y(lambda e: e.dma_start(out=acc[0:NE, 0:DFF], in_=b1_h), dma=True, writes=["acc"])
        for c in range(32):
            cc = c % 4
            K.t(lambda e, c=c, cc=cc: (e.transpose(
                P2[0:P, cc * NE:(cc + 1) * NE], acc[0:NE, c * P:(c + 1) * P],
                identf[0:NE, 0:NE]), e.drain())[1],
                reads=["acc", "identf"], writes=["P2"])
            K.v(lambda e, c=c, cc=cc: e.tensor_copy(
                out=b1T[:, c * NE:(c + 1) * NE], in_=P2[0:P, cc * NE:(cc + 1) * NE]),
                reads=["P2"], writes=["b1T"])
        K.g(lambda e: e.memset(z_sb[:], 0.0), writes=["z_sb"])

    def phaseA(g):
        seq, t0 = divmod(g * G, T)
        for k in range(8):
            K.g(lambda e, k=k: e.dma_start(out=wqkvT[:, k * 3072:(k + 1) * 3072],
                                           in_=wqkvT_h[k * P:(k + 1) * P, :]),
                dma=True, writes=["wqkvT"])
        def emit_qkv_tile(ti):
            tc = ti * 512
            pend = None
            for m in range(24):
                pm, pr = (P0, "P0") if m % 2 == 0 else (P1, "P1")
                for k in range(8):
                    K.t(lambda e, m=m, k=k, pm=pm, tc=tc: e.matmul(
                        pm[:], wqkvT[:, k * 3072 + m * P:k * 3072 + (m + 1) * P],
                        xn_fm[:, k * G + tc:k * G + tc + 512],
                        start=(k == 0), stop=(k == 7)),
                        reads=["wqkvT", "xn_fm"], writes=[pr])
                if pend is not None:
                    vb0, sq0, sq0r, m0 = pend
                    pend = None
                    K.t(lambda e, vb0=vb0, sq0=sq0, m0=m0: e.matmul(
                        P2[0:2 * H, 0:512], hm[:, vb0:vb0 + 32], sq0[:, 0:512],
                        start=(m0 == 0), stop=(m0 == 15)),
                        reads=[sq0r, "hm"], writes=["P2"])
                sect, j = divmod(m, 8)
                if sect == 2:
                    K.v(lambda e, j=j, pm=pm, tc=tc: e.tensor_copy(
                        out=v_pool[:, j * G + tc:j * G + tc + 512], in_=pm[:]),
                        reads=[pr], writes=["v_sb"])
                else:
                    qk = q_pool if sect == 0 else k_pool
                    qkr = "q_sb" if sect == 0 else "k_sb"
                    K.v(lambda e, j=j, pm=pm, tc=tc, qk=qk: e.tensor_copy(
                        out=qk[:, j * G + tc:j * G + tc + 512], in_=pm[:]),
                        reads=[pr], writes=[qkr])
                    sq, sqr = (sqs, "sqs") if m % 2 == 0 else (sqs2, "sqs2")
                    K.s(lambda e, j=j, tc=tc, qk=qk, sq=sq: e.activation(
                        out=sq[:, 0:512], in_=qk[:, j * G + tc:j * G + tc + 512],
                        func=AF.Square),
                        reads=[qkr], writes=[sqr])
                    pend = ((sect * 8 + j) * 32, sq, sqr, m)
            K.v(lambda e: e.tensor_scalar_max(out=ss_sb[:], in0=P2[0:2 * H, 0:512],
                                              scalar1=1e-24),
                reads=["P2"], writes=["ss_sb"])
            K.s(lambda e: e.activation(out=junk[0:2 * H, 0:512], in_=ss_sb[:],
                                       func=AF.Ln),
                reads=["ss_sb"], writes=["junk"])
            K.s(lambda e: e.activation(out=nf_bf[:], in_=junk[0:2 * H, 0:512],
                                       func=AF.Exp, scale=cc[0:2 * H, 1:2]),
                reads=["junk", "cc"], writes=["nf_bf"])
            for idx in range(16):
                sect, j = divmod(idx, 8)
                qk = q_pool if sect == 0 else k_pool
                qkr = "q_sb" if sect == 0 else "k_sb"
                pmb, pmbr = (P3, "P3") if idx % 2 == 0 else (P4, "P4")
                K.t(lambda e, idx=idx, pmb=pmb: e.matmul(
                    pmb[:], maskT[:, idx * P:(idx + 1) * P], nf_bf[:],
                    start=True, stop=True),
                    reads=["nf_bf", "maskT"], writes=[pmbr])
                K.v(lambda e, j=j, tc=tc, qk=qk, pmb=pmb: e.tensor_tensor(
                    out=qk[:, j * G + tc:j * G + tc + 512],
                    in0=qk[:, j * G + tc:j * G + tc + 512], in1=pmb[:], op=ALU.mult),
                    reads=[pmbr, qkr], writes=[qkr])
            for k in range(8):
                K.t(lambda e, k=k, tc=tc: e.matmul(
                    P4[0:H, 0:512], wgwbT[:, k * 2 * H:k * 2 * H + H],
                    xn_fm[:, k * G + tc:k * G + tc + 512],
                    start=(k == 0), stop=(k == 7)),
                    reads=["wgwbT", "xn_fm"], writes=["P4"])
                K.t(lambda e, k=k, tc=tc: e.matmul(
                    P5[0:H, 0:512], wgwbT[:, k * 2 * H + H:(k + 1) * 2 * H],
                    xn_fm[:, k * G + tc:k * G + tc + 512],
                    start=(k == 0), stop=(k == 7)),
                    reads=["wgwbT", "xn_fm"], writes=["P5"])
            K.s(lambda e, tc=tc: e.activation(out=alpha[:, tc:tc + 512],
                                              in_=P4[0:H, 0:512],
                                              func=AF.Sigmoid, bias=bg_sb[:]),
                reads=["P4", "bg_sb"], writes=["alpha"])
            K.s(lambda e, tc=tc: e.activation(out=beta[:, tc:tc + 512],
                                              in_=P5[0:H, 0:512],
                                              func=AF.Sigmoid, bias=bb_sb[:]),
                reads=["P5", "bb_sb"], writes=["beta"])

        for ti_x in range(NTILE):
          K.y(lambda e, seq=seq, t0=t0, ti_x=ti_x: e.dma_start(
            out=x_tok[:].rearrange("p (s d) -> p s d", d=D),
            in_=x_h[seq, t0 + ti_x * 512:t0 + (ti_x + 1) * 512, :].rearrange("(s p) d -> p s d", p=P)),
            dma=True, writes=["x_tok"])
          if DEBUG and g == 0 and ti_x == 0:
            K.g(lambda e: e.dma_start(out=dbg["d_xtok"], in_=x_tok[:]), dma=True, reads=["x_tok"])
          for sl in range(4):
            s = ti_x * 4 + sl
            xs = x_tok[:, sl * D:(sl + 1) * D]
            K.v(lambda e, xs=xs: e.reduce_sum(out=ln_scr[:, 0:1], in_=xs,
                                              axis=mybir.AxisListType.X),
                reads=["x_tok"], writes=["ln0"])
            K.s(lambda e, xs=xs: e.activation(
                out=junk[:, 0:D], in_=xs, func=AF.Square, accum_out=ln_scr[:, 1:2]),
                reads=["x_tok"], writes=["junk", "ln1"])
            K.v(lambda e: e.tensor_scalar_mul(out=ln_scr[:, 2:3], in0=ln_scr[:, 0:1],
                                              scalar1=1.0 / D),
                reads=["ln0"], writes=["ln2"])
            K.v(lambda e: e.tensor_tensor(out=ln_scr[:, 3:4], in0=ln_scr[:, 2:3],
                                          in1=ln_scr[:, 2:3], op=ALU.mult),
                reads=["ln2"], writes=["ln3"])
            K.v(lambda e: e.scalar_tensor_tensor(out=ln_scr[:, 4:5], in0=ln_scr[:, 1:2],
                                                 scalar=1.0 / D, in1=ln_scr[:, 3:4],
                                                 op0=ALU.mult, op1=ALU.subtract),
                reads=["ln1", "ln3"], writes=["ln4"])
            K.s(lambda e: e.activation(out=ln_scr[:, 5:6], in_=ln_scr[:, 4:5],
                                       func=AF.Ln, bias=cc[:, 0:1]),
                reads=["ln4", "cc"], writes=["ln5"])
            K.s(lambda e: e.activation(out=ln_scr[:, 6:7], in_=ln_scr[:, 5:6],
                                       func=AF.Exp, scale=cc[:, 1:2]),
                reads=["ln5", "cc"], writes=["ln6"])
            if DEBUG and g == 0 and s == 0:
                K.g(lambda e: e.dma_start(out=dbg["d_lns"], in_=ln_scr[:]), dma=True,
                    reads=["ln0", "ln1", "ln2", "ln3", "ln4", "ln5", "ln6"])
            xnt = xn_tok if s % 2 == 0 else xn_tok2
            xntr = "xn_tok" if s % 2 == 0 else "xn_tok2"
            K.v(lambda e, xs=xs, xnt=xnt: e.tensor_scalar(out=xnt[:, 0:D], in0=xs,
                                                 scalar1=ln_scr[:, 2:3],
                                                 scalar2=ln_scr[:, 6:7],
                                                 op0=ALU.subtract, op1=ALU.mult),
                reads=["x_tok", "ln2", "ln6"], writes=[xntr])
            for k in range(8):
                pbuf, pr = (PB6, "P6") if k % 2 == 0 else (PB7, "P7")
                K.t(lambda e, k=k, pbuf=pbuf, xnt=xnt: (e.transpose(
                    pbuf[:, 0:P], xnt[:, k * P:(k + 1) * P], ident_bf[:]), e.drain())[1],
                    reads=[xntr, "ident_bf"], writes=[pr])
                K.v(lambda e, k=k, s=s, pbuf=pbuf: e.tensor_copy(
                    out=xn_fm[:, k * G + s * P:k * G + (s + 1) * P], in_=pbuf[:, 0:P]),
                    reads=[pr], writes=["xn_fm"])
          if sl == 3:
            emit_qkv_tile(ti_x)

        K.s(lambda e: e.activation(out=junk[0:H, 0:G], in_=alpha[:], func=AF.Ln),
            reads=["alpha"], writes=["junk"])
        K.v(lambda e: e.tensor_tensor_scan(out=cl[:], data0=junk[0:H, 0:G],
                                           data1=zeros16[:], initial=0.0,
                                           op0=ALU.add, op1=ALU.add),
            reads=["junk", "zeros16"], writes=["cl"])

    def chunk_pre(g, c):
        cb = c % 2
        st_clg, st_lam, st_gcb = st_clg2[cb], st_lam2[cb], st_gcb2[cb]
        kvt_ch = kvt_ch2[cb]
        c0 = c * P
        if c == 0:
            prevcol, prr = zeros16[:, 0:1], "zeros16"
        else:
            prevcol, prr = cl[:, c0 - 1:c0], "cl"
        K.v(lambda e, c0=c0, prevcol=prevcol: e.tensor_scalar(
            out=st_clg[:], in0=cl[:, c0:c0 + P], scalar1=prevcol, scalar2=None,
            op0=ALU.subtract),
            reads=["cl", prr], writes=[f"st_clg{cb}"])
        K.v(lambda e: e.tensor_scalar(out=st_lam[:], in0=st_clg[:],
                                      scalar1=st_clg[:, P - 1:P], scalar2=None,
                                      op0=ALU.subtract),
            reads=[f"st_clg{cb}"], writes=[f"st_lam{cb}"])
        K.v(lambda e: e.tensor_scalar(out=st_gcb[:], in0=zeros16[:, 0:P],
                                      scalar1=st_clg[:, P - 1:P], scalar2=None,
                                      op0=ALU.add),
            reads=[f"st_clg{cb}", "zeros16"], writes=[f"st_gcb{cb}"])
        srcs = [(st_clg[:], f"st_clg{cb}"), (st_lam[:], f"st_lam{cb}"),
                (st_gcb[:], f"st_gcb{cb}"), (beta[:, c0:c0 + P], "beta")]
        for i, (sl, rn) in enumerate(srcs):
            K.t(lambda e, i=i, sl=sl: (e.transpose(
                P2[0:P, i * H:(i + 1) * H], sl, identf[0:H, 0:H]), e.drain())[1],
                reads=[rn, "identf"], writes=["P2"])
        K.v(lambda e, cb=cb: e.tensor_copy(out=colT[cb][:], in_=P2[0:P, 0:64]),
            reads=["P2"], writes=[f"colT{cb}"])
        dc = dcol[cb]
        dcr = f"dcol{cb}"
        ctr = f"colT{cb}"
        ct = colT[cb]
        K.v(lambda e, dc=dc, ct=ct: e.tensor_scalar_mul(
            out=dc[:, 0:H], in0=ct[:, 0:H], scalar1=-1.0),
            reads=[ctr], writes=[dcr])
        K.s(lambda e, dc=dc, ct=ct: e.activation(out=dc[:, H:2 * H], in_=ct[:, 0:H],
                                                 func=AF.Exp),
            reads=[ctr], writes=[dcr])
        K.v(lambda e, dc=dc: e.tensor_scalar_mul(
            out=dc[:, 2 * H:3 * H], in0=dc[:, H:2 * H], scalar1=-1.0),
            reads=[dcr], writes=[dcr])
        K.s(lambda e, dc=dc, ct=ct: e.activation(out=dc[:, 3 * H:4 * H],
                                                 in_=ct[:, H:2 * H],
                                                 func=AF.Exp, scale=cc[:, 2:3]),
            reads=[ctr, "cc"], writes=[dcr])
        K.v(lambda e, dc=dc, ct=ct: e.tensor_tensor(
            out=dc[:, 3 * H:4 * H], in0=dc[:, 3 * H:4 * H],
            in1=ct[:, 3 * H:4 * H], op=ALU.mult),
            reads=[dcr, ctr], writes=[dcr])
        K.s(lambda e, dc=dc, ct=ct: e.activation(out=dc[:, 4 * H:5 * H],
                                                 in_=ct[:, 2 * H:3 * H], func=AF.Exp),
            reads=[ctr], writes=[dcr])
        K.v(lambda e, cb=cb, ct=ct: e.tensor_scalar_mul(
            out=negb[cb][:], in0=ct[:, 3 * H:4 * H], scalar1=-1.0),
            reads=[ctr], writes=[f"negb{cb}"])
        K.s(lambda e: e.activation(out=zbf[:], in_=z_sb[:], func=AF.Copy),
            reads=["z_sb"], writes=["zbf"])
        for j in range(8):
            pbuf, pr = (PB6, "P6") if j % 2 == 0 else (PB7, "P7")
            K.t(lambda e, j=j, pbuf=pbuf, c0=c0: (e.transpose(
                pbuf[:, 0:P], k_pool[:, j * G + c0:j * G + c0 + P], ident_bf[:]), e.drain())[0],
                reads=["k_sb", "ident_bf"], writes=[pr])
            K.t(lambda e, j=j, pbuf=pbuf, c0=c0: (e.transpose(
                pbuf[:, P:2 * P], v_pool[:, j * G + c0:j * G + c0 + P], ident_bf[:]), e.drain())[1],
                reads=["v_sb", "ident_bf"], writes=[pr])
            K.v(lambda e, j=j, pbuf=pbuf: e.tensor_copy(
                out=kvt_ch[:, j * 2 * P:(j + 1) * 2 * P], in_=pbuf[:, 0:2 * P]),
                reads=[pr], writes=[f"kvt_ch{cb}"])

    def dn_instance_steps(g, c, h):
        S = []
        cb = c % 2
        par = h % 4
        c0 = c * P
        j = h // 2
        pb = (h % 2) * 64
        zcol = (h // 2) * 64
        TA, TAr = [(P0, "P0"), (P1, "P1"), (P2, "P2"), (P6, "P6")][par]
        TB, TBr = [(P3, "P3"), (P4, "P4"), (P5, "P5"), (P7, "P7")][par]
        PBt, PBr = TBV[par], TBr
        kT = k_pool[pb:pb + 64, j * G + c0:j * G + c0 + P]
        qT = q_pool[pb:pb + 64, j * G + c0:j * G + c0 + P]
        dc = dcol[cb]
        dcr = f"dcol{cb}"
        st_clg = st_clg2[cb]
        kvt_ch = kvt_ch2[cb]
        o_tm = o_tm2[cb]
        negcl = dc[:, h:h + 1]
        Gamc = dc[:, H + h:H + h + 1]
        negGam = dc[:, 2 * H + h:2 * H + h + 1]
        lamb = dc[:, 3 * H + h:3 * H + h + 1]
        gamC = dc[:, 4 * H + h:4 * H + h + 1]
        bcol = colT[cb][:, 3 * H + h:3 * H + h + 1]
        nbcol = negb[cb][:, h:h + 1]
        Y = Ysb[par]
        Yr = f"Ysb{par}"
        Pi, Pir = Pincl[par], f"Pincl{par}"
        Pst, Pstr = Pstrict[par], f"Pstrict{par}"
        Et, Etr = ETb[par], f"ETb{par}"
        KK, KQ = TA[:, 0:P], TA[:, P:2 * P]
        CBi, CBs = TA[:, 2 * P:3 * P], TA[:, 3 * P:4 * P]
        S1, S2 = TA[:, 0:P], TA[:, P:2 * P]
        O1, QZ = TA[:, 2 * P:2 * P + 64], TA[:, 2 * P + 64:3 * P]
        ZU = TA[:, 3 * P:3 * P + 64]
        KZ, Yp = TB[:, 0:64], TB[:, 64:P]

        S.append(lambda: K.t(lambda e: e.matmul(KK, kT, kT, start=True, stop=True),
            reads=["k_sb"], writes=[TAr]))
        S.append(lambda: K.t(lambda e: e.matmul(KQ, kT, qT, start=True, stop=True),
            reads=["k_sb", "q_sb"], writes=[TAr]))
        S.append(lambda: K.t(lambda e: e.matmul(CBi, ident_bf[:], mb_incl[:],
                               start=True, stop=False),
            reads=["ident_bf", "mb_incl"], writes=[TAr]))
        S.append(lambda: K.t(lambda e, h=h: e.matmul(CBi, sel[:, h * P:(h + 1) * P], st_clg[:],
                                    start=False, stop=True),
            reads=["sel", f"st_clg{cb}"], writes=[TAr]))
        S.append(lambda: K.s(lambda e: e.activation(out=Pi[:], in_=CBi, func=AF.Exp, bias=negcl),
            reads=[TAr, dcr], writes=[Pir]))
        S.append(lambda: K.v(lambda e: e.tensor_tensor(out=Pst[:], in0=Pi[:], in1=diag0[:],
                                                       op=ALU.mult),
            reads=[Pir, "diag0"], writes=[Pstr]))
        S.append(lambda: K.v(lambda e: e.scalar_tensor_tensor(out=Et[:], in0=Pi[:], scalar=bcol,
                                             in1=KQ, op0=ALU.mult, op1=ALU.mult),
            reads=[Pir, TAr, f"colT{cb}"], writes=[Etr]))
        S.append(lambda: K.v(lambda e: e.scalar_tensor_tensor(out=Xa[par][:, 0:P], in0=Pst[:],
                                             scalar=nbcol, in1=KK,
                                             op0=ALU.mult, op1=ALU.mult),
            reads=[Pstr, TAr, f"negb{cb}"], writes=[f"Xa{par}"]))
        S.append(lambda: K.t(lambda e: e.matmul(KZ, kT, zbf[pb:pb + 64, zcol:zcol + 64],
                               start=True, stop=True),
            reads=["k_sb", "zbf"], writes=[TBr]))
        S.append(lambda: K.v(lambda e: e.scalar_tensor_tensor(out=Y[:], in0=KZ, scalar=negGam,
                                             in1=kvt_ch[:, (h // 2) * 2 * P + P + (h % 2) * 64:(h // 2) * 2 * P + P + (h % 2) * 64 + 64],
                                             op0=ALU.mult, op1=ALU.add),
            reads=[TBr, dcr, f"kvt_ch{cb}"], writes=[Yr]))
        S.append(lambda: K.t(lambda e: (e.transpose(PBt[:, 512:512 + P], Xa[par][:, 0:P], ident_bf[:]), e.drain())[1],
            reads=[f"Xa{par}", "ident_bf"], writes=[PBr]))
        S.append(lambda: K.v(lambda e: e.tensor_copy(out=Xa[par][:, P:2 * P], in_=PBt[:, 512:512 + P]),
            reads=[PBr], writes=[f"Xa{par}"]))
        S.append(lambda: K.t(lambda e: e.matmul(Yp, ident_bf[:], Y[:], start=True, stop=False),
            reads=["ident_bf", Yr], writes=[TBr]))
        X_cur, Xcr = Xa[par], f"Xa{par}"
        X_nxt, Xnr = Xb[par], f"Xb{par}"
        nlev = 7
        for lv in range(nlev):
            S.append(lambda X_cur=X_cur, Xcr=Xcr, lv=lv: K.t(lambda e, X_cur=X_cur, lv=lv: e.matmul(
                Yp, X_cur[:, 0:P], Y[:], start=False, stop=(lv == nlev - 1)),
                reads=[Xcr, Yr], writes=[TBr]))
            if par % 2 == 0:
                S.append(lambda: K.s(lambda e: e.activation(out=Y[:], in_=Yp, func=AF.Copy),
                    reads=[TBr], writes=[Yr]))
            else:
                S.append(lambda: K.v(lambda e: e.tensor_copy(out=Y[:], in_=Yp),
                    reads=[TBr], writes=[Yr]))
            if lv < nlev - 1:
                S.append(lambda X_cur=X_cur, Xcr=Xcr: K.t(lambda e, X_cur=X_cur: e.matmul(
                    S1, X_cur[:, P:2 * P], X_cur[:, 0:P], start=True, stop=True),
                    reads=[Xcr], writes=[TAr]))
                if lv < nlev - 2:
                    S.append(lambda X_cur=X_cur, Xcr=Xcr: K.t(lambda e, X_cur=X_cur: e.matmul(
                        S2, X_cur[:, 0:P], X_cur[:, P:2 * P], start=True, stop=True),
                        reads=[Xcr], writes=[TAr]))
                if par % 2 == 1:
                    S.append(lambda X_nxt=X_nxt, Xnr=Xnr: K.s(lambda e, X_nxt=X_nxt: e.activation(
                        out=X_nxt[:], in_=TA[:, 0:2 * P], func=AF.Copy),
                        reads=[TAr], writes=[Xnr]))
                else:
                    S.append(lambda X_nxt=X_nxt, Xnr=Xnr: K.v(lambda e, X_nxt=X_nxt: e.tensor_copy(
                        out=X_nxt[:], in_=TA[:, 0:2 * P]),
                        reads=[TAr], writes=[Xnr]))
                X_cur, X_nxt = X_nxt, X_cur
                Xcr, Xnr = Xnr, Xcr
        S.append(lambda: K.v(lambda e: e.tensor_scalar(out=mlb[par][:], in0=Y[:], scalar1=lamb,
                                      scalar2=None, op0=ALU.mult),
            reads=[Yr, dcr], writes=[f"mlb{par}"]))
        S.append(lambda: K.t(lambda e: e.matmul(O1, Et[:], Y[:], start=True, stop=True),
            reads=[Etr, Yr], writes=[TAr]))
        S.append(lambda: K.t(lambda e: e.matmul(QZ, qT, zbf[pb:pb + 64, zcol:zcol + 64],
                               start=True, stop=True),
            reads=["q_sb", "zbf"], writes=[TAr]))
        S.append(lambda: K.v(lambda e: e.tensor_scalar(out=o2s[par][:], in0=QZ, scalar1=Gamc,
                                      scalar2=None, op0=ALU.mult),
            reads=[TAr, dcr], writes=[f"o2s{par}"]))
        S.append(lambda: K.v(lambda e, h=h: e.tensor_tensor(out=o_tm[:, h * 64:(h + 1) * 64],
                                           in0=O1, in1=o2s[par][:], op=ALU.add),
            reads=[TAr, f"o2s{par}"], writes=[f"o_tm{cb}"]))
        if pb == 0:
            S.append(lambda: K.t(lambda e, h=h: e.matmul(TA[0:64, 3 * P:3 * P + 64],
                                        kvt_ch[:, (h // 2) * 2 * P + (h % 2) * 64:(h // 2) * 2 * P + (h % 2) * 64 + 64],
                                        mlb[par][:], start=True, stop=True),
                reads=[f"kvt_ch{cb}", f"mlb{par}"], writes=[TAr]))
        else:
            S.append(lambda: K.t(lambda e, h=h: e.matmul(TA[64:P, 3 * P:3 * P + 64],
                                        kvt_ch[:, (h // 2) * 2 * P + (h % 2) * 64:(h // 2) * 2 * P + (h % 2) * 64 + 64],
                                        mlb[par][:], start=True, stop=True,
                                        tile_position=(0, 64)),
                reads=[f"kvt_ch{cb}", f"mlb{par}"], writes=[TAr]))
        S.append(lambda: K.v(lambda e, pb=pb, zcol=zcol: e.scalar_tensor_tensor(
            out=z_sb[pb:pb + 64, zcol:zcol + 64],
            in0=z_sb[pb:pb + 64, zcol:zcol + 64],
            scalar=gamC[pb:pb + 64, :], in1=TA[pb:pb + 64, 3 * P:3 * P + 64],
            op0=ALU.mult, op1=ALU.add),
            reads=["z_sb", dcr, TAr], writes=["z_sb"]))
        return S

    def chunk_post(g, c):
        cb = c % 2
        o_tm = o_tm2[cb]
        c0 = c * P
        for j in range(8):
            pbuf, pr = (PB6, "P6") if j % 2 == 0 else (PB7, "P7")
            K.t(lambda e, j=j, pbuf=pbuf: (e.transpose(
                pbuf[:, 0:P], o_tm[:, j * P:(j + 1) * P], ident_bf[:]), e.drain())[1],
                reads=[f"o_tm{cb}", "ident_bf"], writes=[pr])
            K.s(lambda e, j=j, pbuf=pbuf, c0=c0: e.activation(
                out=o_fm[:, j * G + c0:j * G + c0 + P], in_=pbuf[:, 0:P], func=AF.Copy),
                reads=[pr], writes=["o_fm"])

    def phaseB(g):
        if g % (T // G) == 0:
            K.g(lambda e: e.memset(z_sb[:], 0.0), reads=["z_sb"], writes=["z_sb"])
        for c in range(NCH):
            chunk_pre(g, c)
            for h0 in range(0, H, 4):
                hs = [hh for hh in range(h0, min(h0 + 4, H))]
                SL = [dn_instance_steps(g, c, hh) for hh in hs]
                n = max(len(s_) for s_ in SL)
                for i2 in range(n):
                    for s_ in SL:
                        if i2 < len(s_):
                            s_[i2]()
            for h in []:
                if DEBUG and g == 0 and c == 0 and h == 4:
                    K.g(lambda e: e.dma_start(out=dbg["d_et"], in_=ETb[0][:]), dma=True, reads=["ETb0"])
                    K.g(lambda e: e.dma_start(out=dbg["d_y"], in_=Ysb[0][:]), dma=True, reads=["Ysb0"])
            if DEBUG and g == 0 and c == 0:
                K.g(lambda e: e.dma_start(out=dbg["d_otm"], in_=o_tm2[0][:]), dma=True, reads=["o_tm0"])
            chunk_post(g, c)
            if DEBUG and g == 0 and c == 0:
                K.g(lambda e: e.dma_start(out=dbg["d_ofm0"], in_=o_fm[:]), dma=True, reads=["o_fm"])

    def phaseC(g):
        for half in range(2):
            K.g(lambda e, half=half: e.dma_start(
                out=woutT[:].rearrange("p (k c) -> p k c", c=KB)[:, half * 4:(half + 1) * 4, :],
                in_=woutT_h.rearrange("(k p) c -> p k c", p=P)[:, half * 4:(half + 1) * 4, :]),
                dma=True, writes=["woutT"])
        for tb in range(NSUB):
            (pa, pra), (pb, prb) = (((P0, "P0"), (P1, "P1")) if tb % 2 == 0
                                    else ((P2, "P2"), (P3, "P3")))
            for k in range(8):
                K.t(lambda e, tb=tb, k=k, pa=pa: e.matmul(
                    pa[:], o_fm[:, k * G + tb * P:k * G + (tb + 1) * P],
                    woutT[:, k * KB:k * KB + 512],
                    start=(k == 0), stop=(k == 7)),
                    reads=["o_fm", "woutT"], writes=[pra])
                K.t(lambda e, tb=tb, k=k, pb=pb: e.matmul(
                    pb[:], o_fm[:, k * G + tb * P:k * G + (tb + 1) * P],
                    woutT[:, k * KB + 512:(k + 1) * KB],
                    start=(k == 0), stop=(k == 7)),
                    reads=["o_fm", "woutT"], writes=[prb])
            K.v(lambda e, tb=tb, pa=pa: e.tensor_copy(
                out=acc[:, tb * KB:tb * KB + 512], in_=pa[:]),
                reads=[pra], writes=["acc"])
            K.v(lambda e, tb=tb, pb=pb: e.tensor_copy(
                out=acc[:, tb * KB + 512:(tb + 1) * KB], in_=pb[:]),
                reads=[prb], writes=["acc"])

    def router(g):
        def gates_tail(tb):
            K.t(lambda e, tb=tb: (e.transpose(P3[0:NE, 0:P],
                                             gates[:, tb * NE:(tb + 1) * NE], identf[:]), e.drain())[1],
                reads=["gates", "identf"], writes=["P3"])
            K.v(lambda e: e.tensor_copy(out=g_scr[0:NE, 6 * NE:6 * NE + P],
                                        in_=P3[0:NE, 0:P]),
                reads=["P3"], writes=["gTs"])
            for oh in range(2):
                K.t(lambda e, oh=oh: e.matmul(P4[:, 0:512],
                                              g_scr[0:NE, 6 * NE:6 * NE + P],
                                              b2_sb[:, oh * 512:(oh + 1) * 512],
                                              start=True, stop=True),
                    reads=["gTs", "b2_sb"], writes=["P4"])
                K.v(lambda e, tb=tb, oh=oh: e.tensor_tensor(
                    out=acc[:, tb * KB + oh * 512:tb * KB + (oh + 1) * 512],
                    in0=acc[:, tb * KB + oh * 512:tb * KB + (oh + 1) * 512],
                    in1=P4[:, 0:512], op=ALU.add),
                    reads=["P4", "acc"], writes=["acc"])

        for tb in range(NSUB):
            pl, plr = (P2, "P2") if tb % 2 == 0 else (P5, "P5")
            K.t(lambda e, pl=pl: e.matmul(pl[:, 0:NE], ones1[:], br_bf[:], start=True, stop=False),
                reads=["ones1", "br_bf"], writes=[plr])
            for k in range(8):
                K.t(lambda e, tb=tb, k=k, pl=pl: e.matmul(
                    pl[:, 0:NE], xn_fm[:, k * G + tb * P:k * G + (tb + 1) * P],
                    wr_sb[:, k * NE:(k + 1) * NE], start=False, stop=(k == 7)),
                    reads=["xn_fm", "wr_sb"], writes=[plr])
            if tb >= 1:
                gates_tail(tb - 1)
            K.v(lambda e, pl=pl: e.reduce_max(out=g_col[:, 0:1], in_=pl[:, 0:NE],
                                       axis=mybir.AxisListType.X),
                reads=[plr], writes=["gc0"])
            K.v(lambda e: e.tensor_scalar_mul(out=g_col[:, 1:2], in0=g_col[:, 0:1],
                                              scalar1=-1.0),
                reads=["gc0"], writes=["gc1"])
            K.s(lambda e, pl=pl: e.activation(out=g_scr[:, 0:NE], in_=pl[:, 0:NE], func=AF.Exp,
                                       bias=g_col[:, 1:2]),
                reads=[plr, "gc1"], writes=["gE"])
            K.v(lambda e: e.reduce_sum(out=g_col[:, 2:3], in_=g_scr[:, 0:NE],
                                       axis=mybir.AxisListType.X),
                reads=["gE"], writes=["gc2"])
            K.v(lambda e: e.reduce_max(out=g_col[:, 3:4], in_=g_scr[:, 0:NE],
                                       axis=mybir.AxisListType.X),
                reads=["gE"], writes=["gc3"])
            K.v(lambda e: e.tensor_scalar(out=g_scr[:, NE:2 * NE], in0=g_scr[:, 0:NE],
                                          scalar1=g_col[:, 3:4], scalar2=None,
                                          op0=ALU.is_equal),
                reads=["gE", "gc3"], writes=["gm1"])
            K.v(lambda e: e.scalar_tensor_tensor(out=g_scr[:, 2 * NE:3 * NE],
                                                 in0=g_scr[:, NE:2 * NE], scalar=-1e30,
                                                 in1=g_scr[:, 0:NE],
                                                 op0=ALU.mult, op1=ALU.add),
                reads=["gm1", "gE"], writes=["gE2"])
            K.v(lambda e: e.reduce_max(out=g_col[:, 4:5], in_=g_scr[:, 2 * NE:3 * NE],
                                       axis=mybir.AxisListType.X),
                reads=["gE2"], writes=["gc4"])
            K.v(lambda e: e.tensor_scalar(out=g_scr[:, 3 * NE:4 * NE], in0=g_scr[:, 0:NE],
                                          scalar1=g_col[:, 4:5], scalar2=None,
                                          op0=ALU.is_ge),
                reads=["gE", "gc4"], writes=["gm2"])
            K.v(lambda e: e.tensor_tensor(out=g_scr[:, 4 * NE:5 * NE], in0=g_scr[:, 0:NE],
                                          in1=g_scr[:, 3 * NE:4 * NE], op=ALU.mult),
                reads=["gE", "gm2"], writes=["gkeep"])
            K.v(lambda e: e.reduce_sum(out=g_col[:, 5:6], in_=g_scr[:, 4 * NE:5 * NE],
                                       axis=mybir.AxisListType.X),
                reads=["gkeep"], writes=["gc5"])
            K.v(lambda e: e.scalar_tensor_tensor(out=g_col[:, 6:7], in0=g_col[:, 2:3],
                                                 scalar=1e-9, in1=g_col[:, 5:6],
                                                 op0=ALU.mult, op1=ALU.add),
                reads=["gc2", "gc5"], writes=["gc6"])
            K.v(lambda e: e.reciprocal(out=g_col[:, 7:8], in_=g_col[:, 6:7]),
                reads=["gc6"], writes=["gc7"])
            K.v(lambda e, tb=tb: e.tensor_scalar(out=gates[:, tb * NE:(tb + 1) * NE],
                                                 in0=g_scr[:, 4 * NE:5 * NE],
                                                 scalar1=g_col[:, 7:8], scalar2=None,
                                                 op0=ALU.mult),
                reads=["gkeep", "gc7"], writes=["gates"])
        gates_tail(NSUB - 1)

    W1B = [w1qA, w1qB]
    W2B = [w2qA, w2qB]

    def load_w1q(e_i, fq, b):
        buf = W1B[b]
        for k in range(8):
            K.g(lambda e, e_i=e_i, fq=fq, k=k, buf=buf: e.dma_start(
                out=buf[:, k * KB:(k + 1) * KB],
                in_=w1_h[e_i, k * P:(k + 1) * P, fq * KB:(fq + 1) * KB]),
                dma=True, writes=[f"w1q{'AB'[b]}_k{k}"])

    def load_w2q(e_i, fq, b):
        buf = W2B[b]
        for h in range(2):
            K.g(lambda e, e_i=e_i, fq=fq, h=h, buf=buf: e.dma_start(
                out=buf[:].rearrange("p (c o) -> p c o", o=KB)[:, h * 4:(h + 1) * 4, :],
                in_=w2_h[e_i].rearrange("(c p) o -> p c o", p=P)[:, fq * 8 + h * 4:fq * 8 + (h + 1) * 4, :]),
                dma=True, writes=[f"w2q{'AB'[b]}_h{h}"])

    def expert_quarter(g, e_i, fq):
        qidx = e_i * 4 + fq
        b = qidx % 2
        B = "AB"[b]
        buf1, buf2 = W1B[b], W2B[b]
        if qidx + 1 < NE * 4:
            ne_i, nfq = divmod(qidx + 1, 4)
            load_w1q(ne_i, nfq, (qidx + 1) % 2)
            load_w2q(ne_i, nfq, (qidx + 1) % 2)
        for fc in range(8):
            (pa, pra), (pb, prb) = (((P0, "P0"), (P1, "P1")) if fc % 2 == 0
                                    else ((P5, "P5"), (P7, "P7")))
            for k in range(8):
                lo = k * KB + fc * P
                K.t(lambda e, k=k, lo=lo, pa=pa, buf1=buf1: e.matmul(
                    pa[:], buf1[:, lo:lo + P], xn_fm[:, k * G:k * G + 512],
                    start=(k == 0), stop=(k == 7)),
                    reads=[f"w1q{B}_k{k}", "xn_fm"], writes=[pra])
                K.t(lambda e, k=k, lo=lo, pb=pb, buf1=buf1: e.matmul(
                    pb[:], buf1[:, lo:lo + P], xn_fm[:, k * G + 512:(k + 1) * G],
                    start=(k == 0), stop=(k == 7)),
                    reads=[f"w1q{B}_k{k}", "xn_fm"], writes=[prb])
            bcol = b1T[:, (fq * 8 + fc) * NE + e_i:(fq * 8 + fc) * NE + e_i + 1]
            K.s(lambda e, fc=fc, pa=pa, bcol=bcol: e.activation(
                out=hbuf[:, fc * 512:(fc + 1) * 512], in_=pa[:],
                func=AF.Gelu, bias=bcol),
                reads=[pra, "b1T"], writes=["hbufA"])
            K.s(lambda e, fc=fc, pb=pb, bcol=bcol: e.activation(
                out=hbuf[:, 4096 + fc * 512:4096 + (fc + 1) * 512], in_=pb[:],
                func=AF.Gelu, bias=bcol),
                reads=[prb, "b1T"], writes=["hbufB"])
        for ti in range(2):
            hco = ti * 4096
            hres = "hbufA" if ti == 0 else "hbufB"
            for tbl in range(4):
                tb = ti * 4 + tbl
                (pc, prc), (pd, prd) = (((P2, "P2"), (P3, "P3")) if tbl % 2 == 0
                                        else ((P4, "P4"), (P6, "P6")))
                for fc in range(8):
                    hlo = hco + fc * 512 + tbl * P
                    K.t(lambda e, fc=fc, hlo=hlo, pc=pc, buf2=buf2: e.matmul(
                        pc[:], hbuf[:, hlo:hlo + P], buf2[:, fc * KB:fc * KB + 512],
                        start=(fc == 0), stop=(fc == 7)),
                        reads=[hres, f"w2q{B}_h{fc // 4}"], writes=[prc])
                    K.t(lambda e, fc=fc, hlo=hlo, pd=pd, buf2=buf2: e.matmul(
                        pd[:], hbuf[:, hlo:hlo + P], buf2[:, fc * KB + 512:(fc + 1) * KB],
                        start=(fc == 0), stop=(fc == 7)),
                        reads=[hres, f"w2q{B}_h{fc // 4}"], writes=[prd])
                gc = gates[:, tb * NE + e_i:tb * NE + e_i + 1]
                K.v(lambda e, tb=tb, pc=pc, gc=gc: e.scalar_tensor_tensor(
                    out=acc[:, tb * KB:tb * KB + 512],
                    in0=pc[:], scalar=gc,
                    in1=acc[:, tb * KB:tb * KB + 512],
                    op0=ALU.mult, op1=ALU.add),
                    reads=[prc, "gates", "acc"], writes=["acc"])
                K.v(lambda e, tb=tb, pd=pd, gc=gc: e.scalar_tensor_tensor(
                    out=acc[:, tb * KB + 512:(tb + 1) * KB],
                    in0=pd[:], scalar=gc,
                    in1=acc[:, tb * KB + 512:(tb + 1) * KB],
                    op0=ALU.mult, op1=ALU.add),
                    reads=[prd, "gates", "acc"], writes=["acc"])

    def phaseD(g):
        router(g)
        for e_i in range(NE):
            for fq in range(4):
                expert_quarter(g, e_i, fq)

    def phaseE(g):
        seq, t0 = divmod(g * G, T)
        for ti_x in range(NTILE):
            K.y(lambda e, seq=seq, t0=t0, ti_x=ti_x: e.dma_start(
                out=x_tok[:].rearrange("p (s d) -> p s d", d=D),
                in_=x_h[seq, t0 + ti_x * 512:t0 + (ti_x + 1) * 512, :].rearrange("(s p) d -> p s d", p=P)),
                dma=True, writes=["x_tok"])
            for sl in range(4):
                s = ti_x * 4 + sl
                K.v(lambda e, s=s, sl=sl: e.tensor_tensor(
                    out=x_tok[:, sl * D:(sl + 1) * D], in0=x_tok[:, sl * D:(sl + 1) * D],
                    in1=acc[:, s * KB:(s + 1) * KB], op=ALU.add),
                    reads=["x_tok", "acc"], writes=["x_tok"])
            K.y(lambda e, seq=seq, t0=t0, ti_x=ti_x: e.dma_start(
                out=out_h[seq, t0 + ti_x * 512:t0 + (ti_x + 1) * 512, :].rearrange("(s p) d -> p s d", p=P),
                in_=x_tok[:].rearrange("p (s d) -> p s d", d=D)),
                dma=True, reads=["x_tok"], writes=["outh"])

    init_consts()
    init_small_weights()
    def dump(nm, ap, res):
        if DEBUG:
            K.g(lambda e: e.dma_start(out=dbg[nm], in_=ap), dma=True, reads=[res])

    for g in range(NG if STOP_AFTER is None else 1):
        phaseA(g)
        load_w1q(0, 0, 0)
        if STOP_AFTER == "A":
            dump("d_xnfm", xn_fm[:], "xn_fm")
            dump("d_q", q_pool[:], "q_sb")
            dump("d_k", k_pool[:], "k_sb")
            dump("d_v", v_pool[:], "v_sb")
            dump("d_alpha", alpha[:], "alpha")
            dump("d_beta", beta[:], "beta")
            dump("d_cl", cl[:], "cl")
            break
        if g == 0:
            dump("d_xnfm", xn_fm[:], "xn_fm")
            dump("d_q", q_pool[:], "q_sb")
            dump("d_k", k_pool[:], "k_sb")
            dump("d_v", v_pool[:], "v_sb")
            dump("d_alpha", alpha[:], "alpha")
            dump("d_beta", beta[:], "beta")
            dump("d_cl", cl[:], "cl")
        phaseB(g)
        if g == 0:
            dump("d_ofm", o_fm[:], "o_fm")
            dump("d_z", z_sb[:], "z_sb")
        phaseC(g)
        load_w2q(0, 0, 0)
        if g == 0:
            dump("d_accC", acc[:], "acc")
        phaseD(g)
        if g == 0:
            dump("d_gates", gates[:], "gates")
            dump("d_accD", acc[:], "acc")
        phaseE(g)
    K.v(lambda e: e.engine_nop(), reads=["outh"])

    with nc.Block() as block:
        K.build(block, sems)

    es.close()
    return nc


def kernel(**inputs):
    x = np.ascontiguousarray(inputs["L0_attn_x"], dtype=np.float32)
    wqkvT = np.ascontiguousarray(np.asarray(inputs["L0_attn_Wqkv"]).T)
    wgT = np.ascontiguousarray(np.asarray(inputs["L0_attn_Wg"]).T)
    wbT = np.ascontiguousarray(np.asarray(inputs["L0_attn_Wb"]).T)
    bg = np.ascontiguousarray(np.asarray(inputs["L0_attn_bg"]).reshape(H, 1))
    bb = np.ascontiguousarray(np.asarray(inputs["L0_attn_bb"]).reshape(H, 1))
    woutT = np.ascontiguousarray(np.asarray(inputs["L0_attn_Wout"]).T)
    wr = np.ascontiguousarray(inputs["L0_ffn_Wr"])
    br = np.ascontiguousarray(np.asarray(inputs["L0_ffn_br"]).reshape(1, NE))
    w1 = np.ascontiguousarray(inputs["L0_ffn_W1"])
    b1 = np.ascontiguousarray(inputs["L0_ffn_b1"])
    w2 = np.ascontiguousarray(inputs["L0_ffn_W2"])
    b2 = np.ascontiguousarray(inputs["L0_ffn_b2"])

    nc = build_nc()
    in_maps = []
    for c in range(8):
        in_maps.append({
            "x": np.ascontiguousarray(x[2 * c:2 * c + 2]),
            "wqkvT": wqkvT, "wgT": wgT, "wbT": wbT, "bg": bg, "bb": bb,
            "woutT": woutT, "wr": wr, "br": br,
            "w1": w1, "b1": b1, "w2": w2, "b2": b2,
        })
    res = run_bass_kernel_spmd(nc, in_maps, core_ids=list(range(8)))
    out = np.concatenate([np.asarray(r["out"]) for r in res.results], axis=0)
    return out.reshape(16, T, D).astype(np.float32)
